# version 01
# baseline (speedup 1.0000x reference)
import numpy as np
import ml_dtypes
from contextlib import ExitStack

import concourse.bass as bass
import concourse.mybir as mybir
from concourse.bass_utils import run_bass_kernel_spmd

F32 = mybir.dt.float32
BF16 = mybir.dt.bfloat16
F8 = mybir.dt.float8e4
F32R = mybir.dt.float32r
AF = mybir.ActivationFunctionType
OP = mybir.AluOpType
AX = mybir.AxisListType

B_, T, E, H, D, II = 2, 4096, 2048, 16, 128, 6144
HPC = 4
TPC = 1024
ECH = E // 128
NCH = 8
TCH = T // NCH


def _f32(a):
    return np.ascontiguousarray(np.asarray(a, dtype=np.float32))

def _bf16(a):
    return np.ascontiguousarray(np.asarray(a).astype(ml_dtypes.bfloat16))

def host_constants():
    inv = (1.0 / (1e6 ** (np.arange(0, D, 2, dtype=np.float32) / np.float32(D)))).astype(np.float32)
    f = (np.arange(T, dtype=np.float32)[:, None] * inv[None, :]).astype(np.float32).astype(np.float64)
    cos = np.concatenate([np.cos(f), np.cos(f)], -1)
    sin = np.concatenate([np.sin(f), np.sin(f)], -1)
    s = 1.0 / np.float64(np.float32(np.sqrt(np.float32(D), dtype=np.float32)))
    cosq = _f32((cos * s).T); sinq = _f32((sin * s).T)
    cosk = _f32(cos.T); sink = _f32(sin.T)
    Rm = np.zeros((D, D), np.float32)
    for i in range(64):
        Rm[i, i + 64] = -1.0
    for i in range(64, 128):
        Rm[i, i - 64] = 1.0
    RT = _bf16(Rm.T)
    r = np.arange(128)[:, None]; q = np.arange(512)[None, :]
    tri = np.zeros((128, 4 * 512), np.float32)
    for j in range(4):
        tri[:, j * 512:(j + 1) * 512] = (128 * j + r <= q)
    tri = _bf16(tri)
    cst = np.zeros((128, 8), np.float32)
    cst[:, 0] = 1e-5; cst[:, 1] = 1.0 / E; cst[:, 2] = -1.0; cst[:, 3] = 1.0
    return dict(cosq=cosq, sinq=sinq, cosk=cosk, sink=sink, RT=RT, tri=tri, cst=cst)

def prep_in_maps(inputs):
    cst = host_constants()
    x = np.asarray(inputs["L0_attn_x"], np.float32)
    xT = [np.ascontiguousarray(x[b].T) for b in range(B_)]
    wqT = _f32(np.asarray(inputs["L0_attn_wq"]).T)
    wkT = _f32(np.asarray(inputs["L0_attn_wk"]).T)
    wvT = _f32(np.asarray(inputs["L0_attn_wv"]).T)
    woT = _f32(np.asarray(inputs["L0_attn_wo"]).T)
    perm = np.concatenate([np.arange((4 * (b % 4) + b // 4) * 128,
                                     (4 * (b % 4) + b // 4) * 128 + 128)
                           for b in range(16)])
    woTp = np.ascontiguousarray(woT[perm, :])
    gw = _f32(inputs["L0_ffn_gate_w"]); uw = _f32(inputs["L0_ffn_up_w"])
    dw = _f32(inputs["L0_ffn_down_w"])
    gsT = _f32(np.asarray(inputs["L0_ffn_gate_scale"]).reshape(II // 128, 128).T)
    usT = _f32(np.asarray(inputs["L0_ffn_up_scale"]).reshape(II // 128, 128).T)
    dsT = _f32(np.asarray(inputs["L0_ffn_down_scale"]).reshape(E // 128, 128).T)
    maps = []
    for c in range(8):
        b, g = divmod(c, 4)
        m = dict(cst)
        m["xT"] = xT[b]
        m["wqT"] = np.ascontiguousarray(wqT[:, g * 512:(g + 1) * 512])
        m["wkT"] = np.ascontiguousarray(wkT[:, g * 512:(g + 1) * 512])
        m["wvT"] = np.ascontiguousarray(wvT[:, g * 512:(g + 1) * 512])
        m["woTp"] = woTp
        m["xownT"] = np.ascontiguousarray(xT[b][:, g * TPC:(g + 1) * TPC])
        m["tokb"] = np.array([[g * TPC]], np.int32)
        m["gw"] = gw; m["uw"] = uw; m["dw"] = dw
        m["gsT"] = gsT; m["usT"] = usT; m["dsT"] = dsT
        maps.append(m)
    return maps


ENGINES = ("sync", "tensor", "vector", "scalar", "gpsimd")

class Sem:
    __slots__ = ("h", "n")
    def __init__(self, h):
        self.h = h; self.n = 0

class KB:
    def __init__(self, nc):
        self.nc = nc
        self.es = ExitStack()
        self.ops = {e: [] for e in ENGINES}
        self._guards = {}
    def sem(self, name):
        return Sem(self.es.enter_context(self.nc.semaphore(name)))
    def sb(self, name, shape, dt):
        g = self.nc.sbuf_tensor(name, shape, dt)
        h = g.__enter__(); self._guards[name] = g
        return h
    def free(self, name):
        self._guards.pop(name).__exit__(None, None, None)
    def ps(self, name, shape, dt=F32):
        g = self.nc.psum_tensor(name, shape, dt)
        h = g.__enter__(); self._guards[name] = g
        return h
    def emit_all(self):
        with self.nc.Block() as blk:
            for e in ENGINES:
                getattr(blk, e)(lambda eng, e=e: [f(eng) for f in self.ops[e]])
        for g in reversed(list(self._guards)):
            self.free(g)
        self.es.close()

def mk(kb, eng, fn, waits=(), inc=None, k=1):
    ws = tuple((s.h, int(n)) for (s, n) in waits)
    ih = None
    if inc is not None:
        inc.n += k; ih = inc.h
    def run(engine, fn=fn, ws=ws, ih=ih, k=k):
        for (h, n) in ws:
            engine.wait_ge(h, n)
        r = fn(engine)
        if ih is not None:
            r.then_inc(ih, k)
    kb.ops[eng].append(run)
    return inc.n if inc is not None else None

def ap3(t2, outer_step, outer_n, width, col0=0, part0=0, parts=128):
    a = t2[:, :] if hasattr(t2, "__getitem__") else t2
    row = a.ap[0][0]
    off = a.offset + col0 + part0 * row
    return bass.AP(a.tensor, off, [[row, parts], [outer_step, outer_n], [1, width]])


def build_nc(debug=False):
    nc = bass.Bass("TRN2", target_bir_lowering=False, dynamic_dma_scratch_size=16384)
    kb = KB(nc)

    xT = nc.dram_tensor("xT", [E, T], F32, kind="ExternalInput").ap()
    wqT = nc.dram_tensor("wqT", [E, 512], F32, kind="ExternalInput").ap()
    wkT = nc.dram_tensor("wkT", [E, 512], F32, kind="ExternalInput").ap()
    wvT = nc.dram_tensor("wvT", [E, 512], F32, kind="ExternalInput").ap()
    gw = nc.dram_tensor("gw", [E, II], F32, kind="ExternalInput").ap()
    uw = nc.dram_tensor("uw", [E, II], F32, kind="ExternalInput").ap()
    dw = nc.dram_tensor("dw", [II, E], F32, kind="ExternalInput").ap()
    gsT = nc.dram_tensor("gsT", [128, II // 128], F32, kind="ExternalInput").ap()
    usT = nc.dram_tensor("usT", [128, II // 128], F32, kind="ExternalInput").ap()
    dsT = nc.dram_tensor("dsT", [128, E // 128], F32, kind="ExternalInput").ap()
    cosq = nc.dram_tensor("cosq", [128, T], F32, kind="ExternalInput").ap()
    sinq = nc.dram_tensor("sinq", [128, T], F32, kind="ExternalInput").ap()
    cosk = nc.dram_tensor("cosk", [128, T], F32, kind="ExternalInput").ap()
    sink = nc.dram_tensor("sink", [128, T], F32, kind="ExternalInput").ap()
    RT = nc.dram_tensor("RT", [128, 128], BF16, kind="ExternalInput").ap()
    tri = nc.dram_tensor("tri", [128, 4 * 512], BF16, kind="ExternalInput").ap()
    cstd = nc.dram_tensor("cst", [128, 8], F32, kind="ExternalInput").ap()

    v_dram = nc.dram_tensor("v_dram", [T, 512], BF16).ap()
    xownT = nc.dram_tensor("xownT", [E, TPC], F32, kind="ExternalInput").ap()
    woTp = nc.dram_tensor("woTp", [E, E], F32, kind="ExternalInput").ap()
    tokb = nc.dram_tensor("tokb", [1, 1], mybir.dt.int32, kind="ExternalInput").ap()
    og_srcs = [nc.dram_tensor("og_src%d" % i, [128, T], BF16).ap() for i in range(HPC)]
    og_dsts = [nc.dram_tensor("og_dst%d" % i, [HPC * 128, T], BF16).ap() for i in range(HPC)]
    yo_dram = nc.dram_tensor("yo_dram", [E, TPC], F32).ap()
    st_dram = nc.dram_tensor("st_dram", [E, TPC], F32).ap()
    y_dram = nc.dram_tensor("y_dram", [E, TPC], F32).ap()
    q_dram = nc.dram_tensor("q_dram", [128, HPC * T], BF16).ap()
    k_dram = nc.dram_tensor("k_dram", [128, HPC * T], BF16).ap()
    outT = nc.dram_tensor("outT", [E, TPC], F32, kind="ExternalOutput").ap()
    dbg = {}
    if debug:
        dbg["qT"] = nc.dram_tensor("dbg_qT", [128, HPC * T], BF16, kind="ExternalOutput").ap()
        dbg["kT"] = nc.dram_tensor("dbg_kT", [128, HPC * T], BF16, kind="ExternalOutput").ap()
        dbg["bnd"] = nc.dram_tensor("dbg_bnd", [128, HPC], F32, kind="ExternalOutput").ap()
        dbg["v"] = nc.dram_tensor("dbg_v", [T, 512], BF16, kind="ExternalOutput").ap()

        dbg["oT"] = nc.dram_tensor("dbg_oT", [HPC * 128, T], BF16, kind="ExternalOutput").ap()
        dbg["stream"] = nc.dram_tensor("dbg_stream", [E, TPC], F32, kind="ExternalOutput").ap()
        dbg["yo"] = nc.dram_tensor("dbg_yo", [E, TPC], F32, kind="ExternalOutput").ap()
        dbg["ogd"] = nc.dram_tensor("dbg_ogd", [HPC * 128, T], BF16, kind="ExternalOutput").ap()
        dbg["sx"] = nc.dram_tensor("dbg_sx", [1, 2 * TPC], F32, kind="ExternalOutput").ap()
        dbg["rows"] = nc.dram_tensor("dbg_rows", [1, 2048], F32, kind="ExternalOutput").ap()
        dbg["fuse"] = nc.dram_tensor("dbg_fuse", [128, 48 * TPC], BF16, kind="ExternalOutput").ap()
        dbg["h2"] = nc.dram_tensor("dbg_h2", [128, ECH * TPC], BF16, kind="ExternalOutput").ap()
        dbg["c3"] = nc.dram_tensor("dbg_c3", [128, 3 * TPC], F32, kind="ExternalOutput").ap()
        dbg["c4"] = nc.dram_tensor("dbg_c4", [128, 4 * 512], F32, kind="ExternalOutput").ap()
        dbg["g8"] = nc.dram_tensor("dbg_g8", [128, 4096], F8, kind="ExternalOutput").ap()
        dbg["h2b"] = nc.dram_tensor("dbg_h2b", [128, 2048], BF16, kind="ExternalOutput").ap()
        dbg["r2"] = nc.dram_tensor("dbg_r2", [128, 2 * TPC], F32, kind="ExternalOutput").ap()
        dbg["y"] = nc.dram_tensor("dbg_y", [E, TPC], F32, kind="ExternalOutput").ap()

    s_pe = kb.sem("s_pe")
    s_ve = kb.sem("s_ve")
    s_ac = kb.sem("s_ac")
    s_pre = kb.sem("s_pre")
    s_w = [kb.sem("s_w%d" % i) for i in range(3)]
    s_x = [kb.sem("s_x%d" % i) for i in range(2)]
    s_r = [kb.sem("s_r%d" % i) for i in range(2)]
    s_vs = [kb.sem("s_vs%d" % i) for i in range(2)]
    s_qs = [kb.sem("s_qs%d" % i) for i in range(2)]
    s_fin = kb.sem("s_fin")

    cst_sb = kb.sb("cst_sb", [128, 8], F32)
    ones_bf = kb.sb("ones_bf", [128, 1], BF16)
    ones_f = kb.sb("ones_f", [1, 128], F32)
    bnd_sb = kb.sb("bnd_sb", [128, HPC], F32)
    w_sb = kb.sb("w_sb", [128, 3 * ECH * 512], BF16)
    RT_sb = kb.sb("RT_sb", [128, 128], BF16)
    rows = kb.sb("rows", [1, 512 * 4], F32)
    nmax = kb.sb("nmax", [1, 64], F32)
    RM = dict(m=rows[0:1, 0:512], e=rows[0:1, 512:1024], v=rows[0:1, 512:1024],
              r=rows[0:1, 1024:1536], mr=rows[0:1, 1536:2048], t=rows[0:1, 1024:1536])

    for col, val in ((0, 1e-5), (1, 1.0 / E), (2, -1.0), (3, 1.0), (4, 0.0)):
        nc.const_aps.register(F32, float(val), cst_sb[:, col:col + 1])

    mk(kb, "sync", lambda e: e.dma_start(out=cst_sb[:, :], in_=cstd[:, :]), inc=s_pre, k=16)
    mk(kb, "sync", lambda e: e.dma_start(out=RT_sb[:, :], in_=RT[:, :]), inc=s_pre, k=16)
    PRE = (s_pre, 32)
    mk(kb, "vector", lambda e: e.memset(ones_bf[:, :], 1.0), inc=s_ve)
    mk(kb, "vector", lambda e: e.memset(ones_f[:, :], 1.0), inc=s_ve)

    wstage = kb.sb("wstage", [128, ECH * 512], F32)
    prev_cast = 0
    for p, wsrc in enumerate((wqT, wkT, wvT)):
        mk(kb, "sync",
           lambda e, wsrc=wsrc: e.dma_start(
               out=ap3(wstage, 512, ECH, 512),
               in_=bass.AP(wsrc.tensor, 0, [[512, 128], [128 * 512, ECH], [1, 512]])),
           waits=[(s_ve, prev_cast)], inc=s_w[p], k=16)
        prev_cast = mk(kb, "vector",
                       lambda e, p=p: e.tensor_copy(w_sb[:, p * ECH * 512:(p + 1) * ECH * 512], wstage[:, :]),
                       waits=[(s_w[p], 16)], inc=s_ve)
    v_wcast = s_ve.n
    kb.free("wstage")

    xbuf = [kb.sb("xbuf0", [128, ECH * TCH], F32), kb.sb("xbuf1", [128, ECH * TCH], F32)]
    ropeb = [kb.sb("ropeb0", [128, 4 * TCH], F32), kb.sb("ropeb1", [128, 4 * TCH], F32)]
    bufA = kb.sb("bufA", [128, ECH * TCH], BF16)
    bufB = kb.sb("bufB", [128, ECH * TCH], BF16)
    qraw = kb.sb("qraw", [128, HPC * TCH], BF16)
    qstg = kb.sb("qstg", [128, HPC * 512], BF16)
    qsq = kb.sb("qsq", [128, HPC * TCH], BF16)
    vst = kb.sb("vst", [128, 4 * 512], BF16)
    rtmp = kb.sb("rtmp", [128, 2 * TCH], BF16)

    psStatA = kb.ps("psStatA", [1, 512])
    psStatB = kb.ps("psStatB", [1, 512])
    psAux = [kb.ps("psAux0", [128, 512]), kb.ps("psAux1", [128, 512])]
    psProj = kb.ps("psProj", [128, 4 * 512])
    Rbc = kb.sb("Rbc", [128, 512], F32)
    MRbc = kb.sb("MRbc", [128, 512], F32)

    last_x_use = [[(s_ve, v_wcast)], [(s_ve, v_wcast)]]
    last_rope_use = [v_wcast, v_wcast]
    last_stat_read = v_wcast
    last_norm_read = 0
    last_aux_read = [v_wcast, v_wcast]
    last_proj_evac = 0
    n_vst = [0, 0]
    n_x = [0, 0]
    n_r = [0, 0]
    n_qs = [0, 0]
    last_qsq_pe = 0
    last_qsq_a = 0
    last_bufA_pe = 0
    last_bufB_pe = 0

    for ci in range(NCH):
        T0 = ci * TCH
        par = ci % 2
        xb, rb = xbuf[par], ropeb[par]
        n_x[par] += 1
        mk(kb, "sync",
           lambda e, T0=T0, xb=xb: e.dma_start(
               out=ap3(xb, TCH, ECH, TCH),
               in_=bass.AP(xT.tensor, T0, [[T, 128], [128 * T, ECH], [1, TCH]])),
           waits=last_x_use[par], inc=s_x[par], k=16)
        LDX = (s_x[par], 16 * n_x[par])
        n_r[par] += 4
        for j, tab in enumerate((cosq, sinq, cosk, sink)):
            mk(kb, "gpsimd",
               lambda e, tab=tab, j=j, T0=T0, rb=rb: e.dma_start(
                   out=rb[:, j * TCH:(j + 1) * TCH], in_=tab[:, T0:T0 + TCH]),
               waits=[(s_ve, last_rope_use[par])] if j == 0 else (),
               inc=s_r[par], k=16)
        LDR = (s_r[par], 16 * n_r[par])
        a_sq = mk(kb, "scalar",
                  lambda e, xb=xb: e.activation(out=bufA[:, :], in_=xb[:, :], func=AF.Square),
                  waits=[LDX, (s_ve, last_stat_read), (s_pe, last_bufA_pe)], inc=s_ac)
        v_cp = mk(kb, "vector", lambda e, xb=xb: e.tensor_copy(bufB[:, :], xb[:, :]),
                  waits=[LDX, (s_ve, last_stat_read), (s_ac, last_qsq_a),
                         (s_pe, last_bufB_pe)], inc=s_ve)
        pe_w = [(s_ac, a_sq), (s_ve, max(v_cp, last_norm_read))]
        for eci in range(ECH):
            mk(kb, "tensor",
               lambda e, eci=eci: e.matmul(psStatA[:, :], ones_bf[:, :],
                                           bufB[:, eci * TCH:(eci + 1) * TCH],
                                           start=(eci == 0), stop=(eci == ECH - 1)),
               waits=pe_w if eci == 0 else ())
        for eci in range(ECH):
            mk(kb, "tensor",
               lambda e, eci=eci: e.matmul(psStatB[:, :], ones_bf[:, :],
                                           bufA[:, eci * TCH:(eci + 1) * TCH],
                                           start=(eci == 0), stop=(eci == ECH - 1)),
               inc=(s_pe if eci == ECH - 1 else None))
        pe_stats = s_pe.n
        last_bufA_pe = pe_stats
        mk(kb, "vector", lambda e: e.tensor_scalar_mul(RM["m"], psStatA[:, :], 1.0 / E),
           waits=[(s_pe, pe_stats), PRE], inc=s_ve)
        mk(kb, "vector", lambda e: e.tensor_scalar_mul(RM["e"], psStatB[:, :], 1.0 / E), inc=s_ve)
        mk(kb, "vector", lambda e: e.tensor_tensor(RM["t"], RM["m"], RM["m"], op=OP.mult), inc=s_ve)
        mk(kb, "vector", lambda e: e.tensor_tensor(RM["v"], RM["e"], RM["t"], op=OP.subtract), inc=s_ve)
        v_var = mk(kb, "vector", lambda e: e.tensor_scalar_max(RM["v"], RM["v"], 0.0), inc=s_ve)
        last_stat_read = v_var
        a_sqrt = mk(kb, "scalar",
                    lambda e: e.activation(out=RM["v"], in_=RM["v"], func=AF.Sqrt,
                                           bias=cst_sb[0:1, 0:1]),
                    waits=[(s_ve, v_var), PRE], inc=s_ac)
        mk(kb, "vector", lambda e: e.reciprocal(RM["r"], RM["v"]),
           waits=[(s_ac, a_sqrt)], inc=s_ve)
        v_rows = mk(kb, "vector", lambda e: e.tensor_tensor(RM["mr"], RM["m"], RM["r"], op=OP.mult), inc=s_ve)
        mk(kb, "tensor", lambda e: e.matmul(psAux[0][:, :], ones_f[:, :], RM["r"], start=True, stop=True),
           waits=[(s_ve, max(v_rows, last_aux_read[0]))])
        pe_bc = mk(kb, "tensor", lambda e: e.matmul(psAux[1][:, :], ones_f[:, :], RM["mr"], start=True, stop=True),
                   waits=[(s_ve, last_aux_read[1])], inc=s_pe)
        mk(kb, "vector", lambda e: e.tensor_copy(Rbc[:, :], psAux[0][:, :]),
           waits=[(s_pe, pe_bc)], inc=s_ve)
        v_bc = mk(kb, "vector", lambda e: e.tensor_copy(MRbc[:, :], psAux[1][:, :]), inc=s_ve)
        last_aux_read[0] = last_aux_read[1] = v_bc
        mk(kb, "vector",
           lambda e, xb=xb: e.tensor_tensor(ap3(bufA, TCH, ECH, TCH), ap3(xb, TCH, ECH, TCH),
                                            bass.AP(Rbc[:, :].tensor, Rbc[:, :].offset,
                                                    [[512, 128], [0, ECH], [1, TCH]]),
                                            op=OP.mult), inc=s_ve)
        v_h = mk(kb, "vector",
                 lambda e: e.tensor_tensor(ap3(bufB, TCH, ECH, TCH), ap3(bufA, TCH, ECH, TCH),
                                           bass.AP(MRbc[:, :].tensor, MRbc[:, :].offset,
                                                   [[512, 128], [0, ECH], [1, TCH]]),
                                           op=OP.subtract), inc=s_ve)
        last_x_use[par] = [(s_ve, v_h), (s_ac, a_sq)]

        for pk, (ddst, ctab_off, nmcell) in enumerate(((q_dram, 0, 0), (k_dram, 2, 4))):
            for f in range(4):
                for eci in range(ECH):
                    mk(kb, "tensor",
                       lambda e, pk=pk, f=f, eci=eci: e.matmul(
                           psProj[:, f * 512:(f + 1) * 512],
                           w_sb[:, (pk * ECH + eci) * 512 + f * 128:(pk * ECH + eci) * 512 + f * 128 + 128],
                           bufB[:, eci * TCH:(eci + 1) * TCH],
                           start=(eci == 0), stop=(eci == ECH - 1)),
                       waits=([(s_ve, v_h), (s_ac, last_proj_evac)] if (f == 0 and eci == 0) else ()),
                       inc=(s_pe if eci == ECH - 1 and f == 3 else None))
            pe_proj = s_pe.n
            a_ev = mk(kb, "scalar", lambda e: e.copy(qraw[:, :], psProj[:, :]),
                      waits=[(s_pe, pe_proj), (s_ve, s_ve.n)], inc=s_ac)
            last_proj_evac = a_ev
            for f in range(4):
                aux = f % 2
                pe_rot = mk(kb, "tensor",
                            lambda e, f=f, aux=aux: e.matmul(psAux[aux][:, :], RT_sb[:, :],
                                                             qraw[:, f * 512:(f + 1) * 512],
                                                             start=True, stop=True),
                            waits=[(s_ac, a_ev), (s_ve, last_aux_read[aux])] + ([PRE] if ci == 0 else []),
                            inc=s_pe)
                mk(kb, "vector",
                   lambda e, aux=aux, rb=rb, o=ctab_off: e.tensor_tensor(
                       rtmp[:, 0:TCH], psAux[aux][:, :], rb[:, (o + 1) * TCH:(o + 2) * TCH], op=OP.mult),
                   waits=[(s_pe, pe_rot), LDR] + ([(s_ac, last_qsq_a), (s_qs[0], 16 * n_qs[0]), (s_qs[1], 16 * n_qs[1])] if f == 0 else []),
                   inc=s_ve)
                mk(kb, "vector",
                   lambda e, f=f, rb=rb, o=ctab_off: e.tensor_tensor(
                       rtmp[:, TCH:2 * TCH], qraw[:, f * 512:(f + 1) * 512],
                       rb[:, o * TCH:(o + 1) * TCH], op=OP.mult), inc=s_ve)
                v_rope = mk(kb, "vector",
                            lambda e, f=f: e.tensor_tensor(
                                qstg[:, f * 512:(f + 1) * 512],
                                rtmp[:, 0:TCH], rtmp[:, TCH:2 * TCH], op=OP.add), inc=s_ve)
                last_aux_read[aux] = v_rope
            v_rope_all = s_ve.n
            n_qs[pk] += 4
            for f4 in range(4):
                mk(kb, "gpsimd",
                   lambda e, ddst=ddst, T0=T0, f4=f4: e.dma_start(
                       out=bass.AP(ddst.tensor, f4 * T + T0, [[HPC * T, 128], [1, 512]]),
                       in_=qstg[:, f4 * 512:(f4 + 1) * 512]),
                   waits=[(s_ve, v_rope_all)] if f4 == 0 else (), inc=s_qs[pk], k=16)
            last_qsq_a = mk(kb, "scalar",
                            lambda e: e.activation(out=qsq[:, :], in_=qstg[:, :], func=AF.Square),
                            waits=[(s_ve, v_rope_all), (s_pe, last_qsq_pe)], inc=s_ac)
            for f in range(4):
                stat = psStatA if pk == 0 else psStatB
                pe_n = mk(kb, "tensor",
                          lambda e, f=f, stat=stat: e.matmul(stat[:, :], ones_bf[:, :],
                                                             qsq[:, f * 512:(f + 1) * 512],
                                                             start=True, stop=True),
                          waits=[(s_ac, last_qsq_a), (s_ve, max(last_stat_read, last_norm_read))],
                          inc=s_pe)
                last_qsq_pe = pe_n
                v_red = mk(kb, "vector",
                           lambda e, stat=stat: e.reduce_max(nmax[0:1, 8:9], stat[:, :], axis=AX.X),
                           waits=[(s_pe, pe_n)], inc=s_ve)
                cell = nmcell + f
                if ci == 0:
                    v_n = mk(kb, "vector",
                             lambda e, cell=cell: e.tensor_copy(nmax[0:1, cell:cell + 1], nmax[0:1, 8:9]),
                             waits=[(s_ve, v_red)], inc=s_ve)
                else:
                    v_n = mk(kb, "vector",
                             lambda e, cell=cell: e.tensor_tensor(nmax[0:1, cell:cell + 1],
                                                                  nmax[0:1, cell:cell + 1],
                                                                  nmax[0:1, 8:9], op=OP.max),
                             waits=[(s_ve, v_red)], inc=s_ve)
                last_norm_read = v_n

        for ts in range(4):
            for eci in range(ECH):
                mk(kb, "tensor",
                   lambda e, ts=ts, eci=eci: e.matmul(
                       psProj[:, ts * 512:(ts + 1) * 512],
                       bufB[:, eci * TCH + ts * 128:eci * TCH + ts * 128 + 128],
                       w_sb[:, (2 * ECH + eci) * 512:(2 * ECH + eci) * 512 + 512],
                       start=(eci == 0), stop=(eci == ECH - 1)),
                   waits=[(s_ac, last_proj_evac)] if (ts == 0 and eci == 0) else (),
                   inc=(s_pe if eci == ECH - 1 else None))
            pe_v = s_pe.n
            a_vev = mk(kb, "scalar",
                       lambda e, ts=ts: e.copy(vst[:, ts * 512:(ts + 1) * 512],
                                               psProj[:, ts * 512:(ts + 1) * 512]),
                       waits=[(s_pe, pe_v), (s_vs[1 - par], 64 * n_vst[1 - par])], inc=s_ac)
            if ts == 3:
                last_proj_evac = a_vev
            mk(kb, "gpsimd",
               lambda e, ts=ts, T0=T0: e.dma_start(
                   out=v_dram[T0 + ts * 128:T0 + ts * 128 + 128, :],
                   in_=vst[:, ts * 512:(ts + 1) * 512]),
               waits=[(s_ac, a_vev)], inc=s_vs[par], k=16)
        n_vst[par] += 1
        last_bufB_pe = s_pe.n

    mk(kb, "vector", lambda e: e.tensor_tensor(nmax[0:1, 16:20], nmax[0:1, 0:4],
                                               nmax[0:1, 4:8], op=OP.mult),
       waits=[(s_ve, s_ve.n)], inc=s_ve)
    a_bs = mk(kb, "scalar", lambda e: e.activation(out=nmax[0:1, 20:24],
                                                   in_=nmax[0:1, 16:20], func=AF.Sqrt),
              waits=[(s_ve, s_ve.n)], inc=s_ac)
    v_bn = mk(kb, "vector", lambda e: e.tensor_scalar_mul(nmax[0:1, 24:28],
                                                          nmax[0:1, 20:24], -1.0),
              waits=[(s_ac, a_bs)], inc=s_ve)
    pe_bb = mk(kb, "tensor", lambda e: e.matmul(psAux[0][:, 0:HPC], ones_f[:, :],
                                                nmax[0:1, 24:24 + HPC], start=True, stop=True),
               waits=[(s_ve, max(v_bn, last_aux_read[0]))], inc=s_pe)
    v_bnd = mk(kb, "vector", lambda e: e.tensor_copy(bnd_sb[:, :], psAux[0][:, 0:HPC]),
               waits=[(s_pe, pe_bb)], inc=s_ve)

    all_dma_done = [(s_qs[0], 16 * n_qs[0]), (s_qs[1], 16 * n_qs[1]),
                    (s_vs[0], 64 * n_vst[0]), (s_vs[1], 64 * n_vst[1])]

    for nm in ("MRbc", "Rbc", "psProj", "psAux1", "psAux0", "psStatB", "psStatA",
               "rtmp", "vst", "qsq", "qstg", "qraw", "bufB", "bufA",
               "ropeb1", "ropeb0", "xbuf1", "xbuf0", "nmax", "rows", "RT_sb", "w_sb"):
        kb.free(nm)

    tri_sb = kb.sb("tri_sb", [128, 4 * 512], BF16)
    s_tri = kb.sem("s_tri")
    qtile = kb.sb("qtile", [128, 2 * 512], BF16)
    kgrp = kb.sb("kgrp", [128, 2 * 512], BF16)
    vgrp = kb.sb("vgrp", [128, 2 * 512], BF16)
    p_sb = kb.sb("p_sb", [128, 3 * 512], BF16)
    drow = kb.sb("drow", [1, 1024], F32)
    Rcp = kb.sb("Rcp", [128, 512], F32)
    ost = kb.sb("ost", [128, 2 * 512], BF16)
    psS = [kb.ps("psS0", [128, 512]), kb.ps("psS1", [128, 512])]
    psO = [kb.ps("psO0", [128, 512]), kb.ps("psO1", [128, 512])]
    psDen = [kb.ps("psDen0", [1, 512]), kb.ps("psDen1", [1, 512])]
    psB = kb.ps("psB", [128, 512])

    s_qt = kb.sem("s_qt")
    s_kg = [kb.sem("s_kg0"), kb.sem("s_kg1")]
    s_vg = [kb.sem("s_vg0"), kb.sem("s_vg1")]
    s_ost = [kb.sem("s_ost0"), kb.sem("s_ost1")]

    BAR = [(s_pe, s_pe.n), (s_ve, s_ve.n), (s_ac, s_ac.n)] + all_dma_done + \
          [(s_x[0], 16 * n_x[0]), (s_x[1], 16 * n_x[1]),
           (s_r[0], 16 * n_r[0]), (s_r[1], 16 * n_r[1])]
    bar_used = {e: False for e in ENGINES}
    mk(kb, "sync", lambda e: e.dma_start(out=tri_sb[:, :], in_=tri[:, :]),
       waits=BAR, inc=s_tri, k=16)
    bar_used["sync"] = True
    def barw(eng, extra=()):
        w = list(extra)
        if not bar_used[eng]:
            w = BAR + w
            bar_used[eng] = True
        return w

    n_qld = 0
    n_kg = [0, 0]; n_vg = [0, 0]; n_ost = [0, 0]
    last_qtile_pe = [0, 0]
    last_kg_pe = [0, 0]
    last_vg_pe = [0, 0]
    last_psS_ac = [0, 0]
    last_p_pe = [0, 0, 0]
    last_den_ve = [0, 0]
    last_O_ve = [0, 0]
    last_psB_ve = 0
    qk_of_p = [0, 0, 0]

    for lh in range(HPC):
        for qt in range(8):
            NK = 4 * qt + 4
            qb = (lh * 8 + qt) % 2
            n_qld += 1
            mk(kb, "gpsimd",
               lambda e, lh=lh, qt=qt, qb=qb: e.dma_start(
                   out=qtile[:, qb * 512:(qb + 1) * 512],
                   in_=q_dram[:, lh * T + qt * 512:lh * T + qt * 512 + 512]),
               waits=barw("gpsimd", [(s_pe, last_qtile_pe[qb])]), inc=s_qt, k=16)
            QLD = (s_qt, 16 * n_qld)
            exp_last = 0
            for g in range(qt + 1):
                gl = g % 2
                n_kg[gl] += 1
                mk(kb, "sync",
                   lambda e, lh=lh, g=g, gl=gl: e.dma_start(
                       out=kgrp[:, gl * 512:(gl + 1) * 512],
                       in_=k_dram[:, lh * T + g * 512:lh * T + g * 512 + 512]),
                   waits=barw("sync", [(s_pe, last_kg_pe[gl])]), inc=s_kg[gl], k=16)
                KLD = (s_kg[gl], 16 * n_kg[gl])
                n_vg[gl] += 1
                mk(kb, "sync",
                   lambda e, lh=lh, g=g, gl=gl: e.dma_start(
                       out=bass.AP(vgrp[:, :].tensor, vgrp[:, :].offset + gl * 512,
                                   [[1024, 128], [128, 4], [1, 128]]),
                       in_=bass.AP(v_dram.tensor, g * 512 * 512 + lh * 128,
                                   [[512, 128], [128 * 512, 4], [1, 128]])),
                   waits=[(s_pe, last_vg_pe[gl])], inc=s_vg[gl], k=16)
                VLD = (s_vg[gl], 16 * n_vg[gl])
                for j in range(4):
                    kt = g * 4 + j
                    sb_ = kt % 2
                    ps_ = kt % 3
                    pe_qk = mk(kb, "tensor",
                               lambda e, gl=gl, j=j, qb=qb, sb_=sb_: e.matmul(
                                   psS[sb_][:, :],
                                   kgrp[:, gl * 512 + j * 128:gl * 512 + j * 128 + 128],
                                   qtile[:, qb * 512:(qb + 1) * 512],
                                   start=True, stop=True),
                               waits=barw("tensor", [KLD, QLD, (s_ac, last_psS_ac[sb_]),
                                                     (s_ve, last_O_ve[qt % 2] if kt == 0 else 0),
                                                     (s_ve, last_den_ve[qt % 2] if kt == 0 else 0)]),
                               inc=s_pe)
                    a_exp = mk(kb, "scalar",
                               lambda e, sb_=sb_, ps_=ps_, lh=lh: e.activation(
                                   out=p_sb[:, ps_ * 512:(ps_ + 1) * 512],
                                   in_=psS[sb_][:, :], func=AF.Exp,
                                   bias=bnd_sb[:, lh:lh + 1]),
                               waits=barw("scalar", [(s_pe, max(pe_qk, last_p_pe[ps_]))]),
                               inc=s_ac)
                    last_psS_ac[sb_] = a_exp
                    qk_of_p[ps_] = a_exp
                    diag = (kt >= 4 * qt)
                    p_ready = [(s_ac, a_exp)]
                    if diag:
                        jj = kt - 4 * qt
                        v_msk = mk(kb, "vector",
                                   lambda e, ps_=ps_, jj=jj: e.tensor_tensor(
                                       p_sb[:, ps_ * 512:(ps_ + 1) * 512],
                                       p_sb[:, ps_ * 512:(ps_ + 1) * 512],
                                       tri_sb[:, jj * 512:(jj + 1) * 512], op=OP.mult),
                                   waits=barw("vector", [(s_ac, a_exp), (s_tri, 16)]), inc=s_ve)
                        p_ready = [(s_ve, v_msk)]
                    mk(kb, "tensor",
                       lambda e, ps_=ps_, qt=qt, first=(kt == 0), lastk=(kt == NK - 1): e.matmul(
                           psDen[qt % 2][:, :], ones_bf[:, :],
                           p_sb[:, ps_ * 512:(ps_ + 1) * 512],
                           start=first, stop=lastk),
                       waits=p_ready)
                    pe_av = mk(kb, "tensor",
                               lambda e, gl=gl, j=j, ps_=ps_, qt=qt,
                                      first=(kt == 0), lastk=(kt == NK - 1): e.matmul(
                                   psO[qt % 2][:, :],
                                   bass.AP(vgrp[:, :].tensor, vgrp[:, :].offset + gl * 512 + j * 128,
                                           [[1024, 128], [1, 128]]),
                                   p_sb[:, ps_ * 512:(ps_ + 1) * 512],
                                   start=first, stop=lastk),
                               waits=[VLD], inc=s_pe)
                    last_p_pe[ps_] = pe_av
                    last_kg_pe[gl] = pe_av
                    last_vg_pe[gl] = pe_av
                    last_qtile_pe[qb] = pe_av
                    exp_last = pe_av
            v_den = mk(kb, "vector",
                       lambda e, qt=qt: e.tensor_copy(drow[0:1, 0:512], psDen[qt % 2][:, :]),
                       waits=barw("vector", [(s_pe, exp_last)]), inc=s_ve)
            last_den_ve[qt % 2] = v_den
            v_rcp = mk(kb, "vector",
                       lambda e: e.reciprocal(drow[0:1, 512:1024], drow[0:1, 0:512]),
                       waits=[(s_ve, v_den)], inc=s_ve)
            pe_b = mk(kb, "tensor",
                      lambda e: e.matmul(psB[:, :], ones_f[:, :], drow[0:1, 512:1024],
                                         start=True, stop=True),
                      waits=[(s_ve, max(v_rcp, last_psB_ve))], inc=s_pe)
            v_rc = mk(kb, "vector",
                      lambda e: e.tensor_copy(Rcp[:, :], psB[:, :]),
                      waits=[(s_pe, pe_b)], inc=s_ve)
            last_psB_ve = v_rc
            ob = qb
            v_oev = mk(kb, "vector",
                       lambda e, qt=qt, ob=ob: e.tensor_tensor(
                           ost[:, ob * 512:(ob + 1) * 512], psO[qt % 2][:, :],
                           Rcp[:, :], op=OP.mult),
                       waits=[(s_ost[ob], 16 * n_ost[ob])], inc=s_ve)
            last_O_ve[qt % 2] = v_oev
            n_ost[ob] += 1
            mk(kb, "gpsimd",
               lambda e, lh=lh, qt=qt, ob=ob: e.dma_start(
                   out=bass.AP(og_srcs[lh].tensor, qt * 512, [[T, 128], [1, 512]]),
                   in_=ost[:, ob * 512:(ob + 1) * 512]),
               waits=[(s_ve, v_oev)], inc=s_ost[ob], k=16)

    attn_done = [(s_ost[0], 16 * n_ost[0]), (s_ost[1], 16 * n_ost[1])]

    for nm in ("psB", "psDen1", "psDen0", "psO1", "psO0", "psS1", "psS0",
               "ost", "Rcp", "drow", "p_sb", "vgrp", "kgrp", "qtile", "s_b_tri_guard"):
        if nm in kb._guards:
            kb.free(nm)
    kb.free("tri_sb")

    s_cc = kb.sem("s_cc")
    s_og = kb.sem("s_og")
    s_tok = kb.sem("s_tok")
    s_sc = kb.sem("s_sc")
    s_wo = [kb.sem("s_wo0"), kb.sem("s_wo1")]
    s_yo = [kb.sem("s_yo0"), kb.sem("s_yo1")]
    s_ld = [kb.sem("s_ld0"), kb.sem("s_ld1")]
    s_st = [kb.sem("s_st0"), kb.sem("s_st1")]
    s_gw = [kb.sem("s_gw0"), kb.sem("s_gw1")]
    s_dw = [kb.sem("s_dw0"), kb.sem("s_dw1")]
    s_yd = [kb.sem("s_yd0"), kb.sem("s_yd1")]
    s_lnd = kb.sem("s_lnd")

    BAR2 = [(s_pe, s_pe.n), (s_ve, s_ve.n), (s_ac, s_ac.n)] + attn_done + \
           [(s_qt, 16 * n_qld),
            (s_kg[0], 16 * n_kg[0]), (s_kg[1], 16 * n_kg[1]),
            (s_vg[0], 16 * n_vg[0]), (s_vg[1], 16 * n_vg[1])]
    bar2_used = {e: False for e in ENGINES}
    def bar2w(eng, extra=()):
        w = list(extra)
        if not bar2_used[eng]:
            w = BAR2 + w
            bar2_used[eng] = True
        return w

    rows2 = kb.sb("rows2", [1, 512 * 4], F32)
    R2 = dict(m=rows2[0:1, 0:512], e=rows2[0:1, 512:1024], v=rows2[0:1, 512:1024],
              r=rows2[0:1, 1024:1536], mr=rows2[0:1, 1536:2048], t=rows2[0:1, 1024:1536])
    Rbc2 = kb.sb("Rbc2", [128, TPC], F32)
    MRbc2 = kb.sb("MRbc2", [128, TPC], F32)
    scl_sb = kb.sb("scl_sb", [128, 2 * 48 + 16], F32)
    ybuf = kb.sb("ybuf", [128, TPC], F32)
    ctmp = kb.sb("ctmp", [128, TPC], F32R)
    sqtmp = kb.sb("sqtmp", [128, TPC], F32R)
    tok_sb = kb.sb("tok_sb", [1, 2], mybir.dt.int32)
    ones_fr = kb.sb("ones_fr", [128, 1], F32R)
    runrows = kb.sb("runrows", [1, 4 * 512], F32)
    mk(kb, "vector", lambda e: e.tensor_copy(ones_fr[:, :], cst_sb[:, 3:4]), inc=s_ve)
    fuse = kb.sb("fuse", [128, 48 * TPC], BF16)

    greg = kb.es.enter_context(nc.gpsimd.register("tokreg"))
    mk(kb, "gpsimd", lambda e: e.dma_start(out=tok_sb[0:1, 0:1], in_=tokb[:, :]),
       waits=bar2w("gpsimd"), inc=s_tok, k=16)
    def _ldreg(e):
        e.wait_ge(s_tok.h, 16)
        e.reg_load(greg, tok_sb[0:1, 0:1])
    kb.ops["gpsimd"].append(_ldreg)
    mk(kb, "sync", lambda e: e.dma_start(out=scl_sb[:, 0:48], in_=gsT[:, :]),
       waits=bar2w("sync"), inc=s_sc, k=16)
    mk(kb, "sync", lambda e: e.dma_start(out=scl_sb[:, 48:96], in_=usT[:, :]), inc=s_sc, k=16)
    mk(kb, "sync", lambda e: e.dma_start(out=scl_sb[:, 96:112], in_=dsT[:, :]), inc=s_sc, k=16)
    for lh in range(HPC):
        mk(kb, "gpsimd",
           lambda e, lh=lh: e.collective_compute(
               "AllGather", OP.bypass,
               replica_groups=[[0, 1, 2, 3], [4, 5, 6, 7]],
               ins=[og_srcs[lh].opt()], outs=[og_dsts[lh].opt()]),
           waits=attn_done if lh == 0 else (), inc=s_cc)

    og_all = kb.sb("og_all", [128, ECH * TPC], BF16)
    wostg = kb.sb("wostg", [128, 2 * 2048], F32)
    wob = kb.sb("wob", [128, 2 * 2048], BF16)
    psSx = [kb.ps("psSx0", [1, 512]), kb.ps("psSx1", [1, 512])]
    psSxx = [kb.ps("psSxx0", [1, 512]), kb.ps("psSxx1", [1, 512])]
    psBB = kb.ps("psBB", [128, 512])
    psY = [kb.ps("psY0", [128, 512]), kb.ps("psY1", [128, 512])]

    def _og_load(e):
        e.wait_ge(s_cc.h, HPC)
        def dmas(base):
            for lh in range(HPC):
                for cp in range(4):
                    e.dma_start(
                        out=bass.AP(og_all[:, :].tensor,
                                    og_all[:, :].offset + (lh * 4 + cp) * TPC,
                                    [[ECH * TPC, 128], [1, TPC]]),
                        in_=bass.AP(og_dsts[lh].tensor, base + cp * 128 * T,
                                    [[T, 128], [1, TPC]]),
                    ).then_inc(s_og.h, 16)
        with e.If_eq(greg, 0):
            dmas(0)
        with e.Else():
            with e.If_eq(greg, TPC):
                dmas(TPC)
            with e.Else():
                with e.If_eq(greg, 2 * TPC):
                    dmas(2 * TPC)
                with e.Else():
                    dmas(3 * TPC)
    kb.ops["gpsimd"].append(_og_load)
    s_og.n += 16 * HPC * 4
    OGL = (s_og, 16 * HPC * 4)

    n_wo = [0, 0]; n_yo = [0, 0]; n_ld = [0, 0]; n_st = [0, 0]
    n_gw = [0, 0]; n_dw = [0, 0]; n_yd = [0, 0]
    last_wocast = [0, 0]
    last_psY_ac = [0, 0]
    last_ybuf_done = [0, 0]
    last_ctmp_pe = [0, 0]
    last_stats_ve = 0

    def emit_rows(sx, sxx, Rdst, MRdst, gate_pe, bb):
        vv = 0
        for th in range(2):
            mk(kb, "vector", lambda e, th=th: e.tensor_scalar_mul(
                R2["m"], runrows[0:1, th * 512:(th + 1) * 512], 1.0 / E), inc=s_ve)
            mk(kb, "vector", lambda e, th=th: e.tensor_scalar_mul(
                R2["e"], runrows[0:1, 1024 + th * 512:1024 + (th + 1) * 512], 1.0 / E), inc=s_ve)
            mk(kb, "vector", lambda e: e.tensor_tensor(R2["t"], R2["m"], R2["m"], op=OP.mult), inc=s_ve)
            mk(kb, "vector", lambda e: e.tensor_tensor(R2["v"], R2["e"], R2["t"], op=OP.subtract), inc=s_ve)
            v_var = mk(kb, "vector", lambda e: e.tensor_scalar_max(R2["v"], R2["v"], 0.0), inc=s_ve)
            a_s = mk(kb, "scalar", lambda e: e.activation(out=R2["v"], in_=R2["v"], func=AF.Sqrt,
                                                          bias=cst_sb[0:1, 0:1]),
                     waits=[(s_ve, v_var)], inc=s_ac)
            mk(kb, "vector", lambda e: e.reciprocal(R2["r"], R2["v"]), waits=[(s_ac, a_s)], inc=s_ve)
            mk(kb, "vector", lambda e: e.tensor_tensor(R2["mr"], R2["m"], R2["r"], op=OP.mult), inc=s_ve)
            for (key, dst) in (("r", Rdst), ("mr", MRdst)):
                pe_b2 = mk(kb, "tensor",
                           lambda e, key=key, bb=bb: e.matmul(bb[:, :], ones_f[:, :], R2[key],
                                                              start=True, stop=True),
                           waits=[(s_ve, s_ve.n)], inc=s_pe)
                vv = mk(kb, "vector",
                        lambda e, dst=dst, th=th, bb=bb: e.tensor_copy(dst[:, th * 512:(th + 1) * 512],
                                                                       bb[:, :]),
                        waits=[(s_pe, pe_b2)], inc=s_ve)
        return vv

    for gc in range(ECH):
        gl = gc % 2
        n_wo[gl] += 1
        mk(kb, "sync",
           lambda e, gc=gc, gl=gl: e.dma_start(
               out=bass.AP(wostg[:, :].tensor, wostg[:, :].offset + gl * 2048,
                           [[4096, 128], [128, 16], [1, 128]]),
               in_=bass.AP(woTp.tensor, gc * 128, [[2048, 128], [128 * 2048, 16], [1, 128]])),
           waits=[(s_ve, last_wocast[gl])], inc=s_wo[gl], k=16)
        v_woc = mk(kb, "vector",
                   lambda e, gl=gl: e.tensor_copy(wob[:, gl * 2048:(gl + 1) * 2048],
                                                  wostg[:, gl * 2048:(gl + 1) * 2048]),
                   waits=bar2w("vector", [(s_wo[gl], 16 * n_wo[gl]),
                                          (s_pe, max(last_ctmp_pe))]),
                   inc=s_ve)
        last_wocast[gl] = v_woc
        for th in range(2):
            for b in range(16):
                mk(kb, "tensor",
                   lambda e, gl=gl, b=b, th=th: e.matmul(
                       psY[th][:, :],
                       wob[:, gl * 2048 + b * 128:gl * 2048 + b * 128 + 128],
                       og_all[:, b * TPC + th * 512:b * TPC + th * 512 + 512],
                       start=(b == 0), stop=(b == 15)),
                   waits=(bar2w("tensor", [(s_ve, v_woc), OGL, (s_ac, last_psY_ac[th])])
                          if b == 0 else ()),
                   inc=(s_pe if b == 15 else None))
            pe_y = s_pe.n
            a_ev = mk(kb, "scalar",
                      lambda e, th=th, gl=gl: e.copy(
                          ybuf[:, th * 512:(th + 1) * 512],
                          psY[th][:, :]),
                      waits=bar2w("scalar", [(s_pe, pe_y), (s_yo[0], 16 * n_yo[0]),
                                             (s_yo[1], 16 * n_yo[1]),
                                             (s_ve, max(last_ybuf_done))]),
                      inc=s_ac)
            last_psY_ac[th] = a_ev
        a_ev2 = last_psY_ac[1]
        v_c = mk(kb, "vector",
                 lambda e, gl=gl: e.tensor_copy(ctmp[:, 0:TPC],
                                                ybuf[:, 0:TPC]),
                 waits=[(s_ac, a_ev2), (s_pe, max(last_ctmp_pe))], inc=s_ve)
        last_ybuf_done[gl] = v_c
        a_sq2 = mk(kb, "scalar",
                   lambda e, gl=gl: e.activation(out=sqtmp[:, 0:TPC],
                                                 in_=ybuf[:, 0:TPC],
                                                 func=AF.Square),
                   waits=[(s_ve, v_c)], inc=s_ac)
        last_sq_prev = a_sq2
        for th in range(2):
            mk(kb, "tensor",
               lambda e, th=th, ps=psSx[th]: e.matmul(
                   ps[:, :], ones_fr[:, :],
                   ctmp[:, th * 512:(th + 1) * 512],
                   start=True, stop=True),
               waits=[(s_ve, v_c), (s_ac, a_sq2), (s_ve, last_stats_ve)])
            mk(kb, "tensor",
               lambda e, th=th, ps=psSxx[th]: e.matmul(
                   ps[:, :], ones_fr[:, :],
                   sqtmp[:, th * 512:(th + 1) * 512],
                   start=True, stop=True),
               inc=(s_pe if th == 1 else None))
        pe_st = s_pe.n
        for th in range(2):
            op0 = "copy" if gc == 0 else "add"
            if gc == 0:
                mk(kb, "vector", lambda e, th=th, ps=psSx[th]: e.tensor_copy(
                    runrows[0:1, th * 512:(th + 1) * 512], ps[:, :]),
                   waits=[(s_pe, pe_st)] if th == 0 else (), inc=s_ve)
                mk(kb, "vector", lambda e, th=th, ps=psSxx[th]: e.tensor_copy(
                    runrows[0:1, 1024 + th * 512:1024 + (th + 1) * 512], ps[:, :]),
                   inc=s_ve)
            else:
                mk(kb, "vector", lambda e, th=th, ps=psSx[th]: e.tensor_tensor(
                    runrows[0:1, th * 512:(th + 1) * 512],
                    runrows[0:1, th * 512:(th + 1) * 512], ps[:, :], op=OP.add),
                   waits=[(s_pe, pe_st)] if th == 0 else (), inc=s_ve)
                mk(kb, "vector", lambda e, th=th, ps=psSxx[th]: e.tensor_tensor(
                    runrows[0:1, 1024 + th * 512:1024 + (th + 1) * 512],
                    runrows[0:1, 1024 + th * 512:1024 + (th + 1) * 512], ps[:, :], op=OP.add),
                   inc=s_ve)
        last_stats_ve = s_ve.n
        last_ctmp_pe[gl] = s_pe.n
        n_yo[gl] += 1
        mk(kb, "gpsimd",
           lambda e, gc=gc, gl=gl: e.dma_start(
               out=bass.AP(yo_dram.tensor, gc * 128 * TPC, [[TPC, 128], [1, TPC]]),
               in_=ybuf[:, 0:TPC]),
           waits=[(s_ac, a_ev2)], inc=s_yo[gl], k=16)
    pe_stats2 = s_pe.n
    last_stats_ve = emit_rows(psSx, psSxx, Rbc2, MRbc2, pe_stats2, psBB)
    if debug:
        mk(kb, "gpsimd", lambda e: e.dma_start(out=dbg["r2"][:, 0:TPC], in_=Rbc2[:, :]),
           waits=[(s_ve, last_stats_ve)], inc=s_fin, k=16)
        mk(kb, "gpsimd", lambda e: e.dma_start(out=dbg["r2"][:, TPC:2 * TPC], in_=MRbc2[:, :]),
           inc=s_fin, k=16)
        mk(kb, "gpsimd", lambda e: e.dma_start(out=dbg["rows"][0:1, 0:2048],
                                               in_=bass.AP(runrows[:, :].tensor, runrows[:, :].offset,
                                                           [[2048, 1], [1, 2048]])),
           inc=s_fin, k=16)

    kb.free("wob"); kb.free("wostg"); kb.free("og_all")
    h2_sb = kb.sb("h2_sb", [128, ECH * TPC], BF16)
    ych = kb.sb("ych", [128, 2 * TPC], F32)
    xch = kb.sb("xch", [128, 2 * TPC], F32)
    tt1 = kb.sb("tt1", [128, TPC], F32)
    tt2 = kb.sb("tt2", [128, TPC], F32)
    yo_done = [(s_yo[0], 16 * n_yo[0]), (s_yo[1], 16 * n_yo[1])]
    if debug:
        yo_done = yo_done + [(s_fin, s_fin.n)]
    last_sq_prev = 0
    last_tt_use = [0, 0]
    for gc in range(ECH):
        gl = gc % 2
        n_ld[gl] += 2
        mk(kb, "gpsimd",
           lambda e, gc=gc, gl=gl: e.dma_start(
               out=ych[:, gl * TPC:(gl + 1) * TPC],
               in_=bass.AP(yo_dram.tensor, gc * 128 * TPC, [[TPC, 128], [1, TPC]])),
           waits=(yo_done if gc < 2 else []) + [(s_ve, last_tt_use[gl])],
           inc=s_ld[gl], k=16)
        mk(kb, "gpsimd",
           lambda e, gc=gc, gl=gl: e.dma_start(
               out=xch[:, gl * TPC:(gl + 1) * TPC],
               in_=bass.AP(xownT.tensor, gc * 128 * TPC, [[TPC, 128], [1, TPC]])),
           inc=s_ld[gl], k=16)
        LD = (s_ld[gl], 16 * n_ld[gl])
        mk(kb, "vector",
           lambda e, gl=gl: e.tensor_tensor(tt1[:, :], ych[:, gl * TPC:(gl + 1) * TPC],
                                            Rbc2[:, :], op=OP.mult),
           waits=[LD, (s_ve, last_stats_ve)], inc=s_ve)
        mk(kb, "vector", lambda e: e.tensor_tensor(tt2[:, :], tt1[:, :], MRbc2[:, :],
                                                   op=OP.subtract), inc=s_ve)
        v_s = mk(kb, "vector",
                 lambda e, gl=gl: e.tensor_tensor(ybuf[:, 0:TPC], tt2[:, :],
                                                  xch[:, gl * TPC:(gl + 1) * TPC], op=OP.add),
                 waits=[(s_st[0], 16 * n_st[0]), (s_st[1], 16 * n_st[1]),
                        (s_ac, last_sq_prev)] +
                       (yo_done if gc < 2 else []), inc=s_ve)
        last_tt_use[gl] = v_s
        v_c = mk(kb, "vector",
                 lambda e, gl=gl: e.tensor_copy(ctmp[:, 0:TPC],
                                                ybuf[:, 0:TPC]),
                 waits=[(s_pe, max(last_ctmp_pe))], inc=s_ve)
        a_sq2 = mk(kb, "scalar",
                   lambda e, gl=gl: e.activation(out=sqtmp[:, 0:TPC],
                                                 in_=ybuf[:, 0:TPC],
                                                 func=AF.Square),
                   waits=[(s_ve, v_c)], inc=s_ac)
        last_sq_prev = a_sq2
        for th in range(2):
            mk(kb, "tensor",
               lambda e, th=th, ps=psSx[th]: e.matmul(
                   ps[:, :], ones_fr[:, :],
                   ctmp[:, th * 512:(th + 1) * 512],
                   start=True, stop=True),
               waits=[(s_ve, v_c), (s_ac, a_sq2), (s_ve, last_stats_ve)])
            mk(kb, "tensor",
               lambda e, th=th, ps=psSxx[th]: e.matmul(
                   ps[:, :], ones_fr[:, :],
                   sqtmp[:, th * 512:(th + 1) * 512],
                   start=True, stop=True),
               inc=(s_pe if th == 1 else None))
        pe_st = s_pe.n
        for th in range(2):
            op0 = "copy" if gc == 0 else "add"
            if gc == 0:
                mk(kb, "vector", lambda e, th=th, ps=psSx[th]: e.tensor_copy(
                    runrows[0:1, th * 512:(th + 1) * 512], ps[:, :]),
                   waits=[(s_pe, pe_st)] if th == 0 else (), inc=s_ve)
                mk(kb, "vector", lambda e, th=th, ps=psSxx[th]: e.tensor_copy(
                    runrows[0:1, 1024 + th * 512:1024 + (th + 1) * 512], ps[:, :]),
                   inc=s_ve)
            else:
                mk(kb, "vector", lambda e, th=th, ps=psSx[th]: e.tensor_tensor(
                    runrows[0:1, th * 512:(th + 1) * 512],
                    runrows[0:1, th * 512:(th + 1) * 512], ps[:, :], op=OP.add),
                   waits=[(s_pe, pe_st)] if th == 0 else (), inc=s_ve)
                mk(kb, "vector", lambda e, th=th, ps=psSxx[th]: e.tensor_tensor(
                    runrows[0:1, 1024 + th * 512:1024 + (th + 1) * 512],
                    runrows[0:1, 1024 + th * 512:1024 + (th + 1) * 512], ps[:, :], op=OP.add),
                   inc=s_ve)
        last_stats_ve = s_ve.n
        last_ctmp_pe[gl] = s_pe.n
        n_st[gl] += 1
        mk(kb, "gpsimd",
           lambda e, gc=gc, gl=gl: e.dma_start(
               out=bass.AP(st_dram.tensor, gc * 128 * TPC, [[TPC, 128], [1, TPC]]),
               in_=ybuf[:, 0:TPC]),
           waits=[(s_ve, v_s)], inc=s_st[gl], k=16)
    pe_stats3 = s_pe.n
    last_stats_ve = emit_rows(psSx, psSxx, Rbc2, MRbc2, pe_stats3, psBB)

    st_done = [(s_st[0], 16 * n_st[0]), (s_st[1], 16 * n_st[1])]
    for gc in range(ECH):
        gl = gc % 2
        n_ld[gl] += 1
        mk(kb, "gpsimd",
           lambda e, gc=gc, gl=gl: e.dma_start(
               out=ych[:, gl * TPC:(gl + 1) * TPC],
               in_=bass.AP(st_dram.tensor, gc * 128 * TPC, [[TPC, 128], [1, TPC]])),
           waits=(st_done if gc < 2 else []) + [(s_ve, last_tt_use[gl])],
           inc=s_ld[gl], k=16)
        LD = (s_ld[gl], 16 * n_ld[gl])
        mk(kb, "vector",
           lambda e, gl=gl: e.tensor_tensor(tt1[:, :], ych[:, gl * TPC:(gl + 1) * TPC],
                                            Rbc2[:, :], op=OP.mult),
           waits=[LD, (s_ve, last_stats_ve)], inc=s_ve)
        mk(kb, "vector",
           lambda e: e.tensor_tensor(tt2[:, :], tt1[:, :], MRbc2[:, :], op=OP.subtract),
           inc=s_ve)
        v_prev = s_ve.n
        a_h2 = mk(kb, "scalar",
                  lambda e, gc=gc: e.copy(h2_sb[:, gc * TPC:(gc + 1) * TPC], tt2[:, :]),
                  waits=[(s_ve, v_prev), (s_ac, 0)], inc=s_ac)
        v_h2 = mk(kb, "vector", lambda e, a_h2=a_h2: e.engine_nop(),
                  waits=[(s_ac, a_h2)], inc=s_ve)
        last_tt_use[gl] = v_h2
        if debug and gc == 0:
            mk(kb, "gpsimd", lambda e: e.dma_start(out=dbg["c3"][:, 0:TPC], in_=tt1[:, :]),
               waits=[(s_ve, v_h2)], inc=s_fin, k=16)
            mk(kb, "gpsimd", lambda e: e.dma_start(out=dbg["c3"][:, TPC:2 * TPC], in_=tt2[:, :]),
               inc=s_fin, k=16)
            mk(kb, "gpsimd", lambda e, gl=gl: e.dma_start(out=dbg["c3"][:, 2 * TPC:3 * TPC],
                                                          in_=ych[:, gl * TPC:(gl + 1) * TPC]),
               inc=s_fin, k=16)
            gc0_dump = [(s_fin, s_fin.n)]
        if debug and gc == 1:
            mk(kb, "vector", lambda e: e.engine_nop(), waits=gc0_dump, inc=s_ve)
    v_h2_all = s_ve.n

    kb.free("tt2"); kb.free("tt1"); kb.free("xch"); kb.free("ych")
    for nm in ("psY1", "psY0", "psBB", "psSxx1", "psSxx0", "psSx1", "psSx0"):
        kb.free(nm)
    psGU = [[kb.ps("psGU%d%d" % (i, j), [128, 512]) for j in range(4)] for i in range(2)]
    gstg = kb.sb("gstg", [128, 2 * 2048], F32)
    g8 = kb.sb("g8", [128, 2 * 2048], F8)
    gwb = kb.sb("gwb", [128, 2 * 2048], BF16)
    gact = kb.sb("gact", [128, TPC], BF16)

    last_gcast = 0
    last_abf = 0
    last_gwb_pe = 0
    last_fuse_ve = [0, 0]
    last_gact_ve = 0
    n_gwl = 0
    for ic in range(48):
        gl = ic % 2
        n_gwl += 2
        mk(kb, "sync",
           lambda e, ic=ic: e.dma_start(
               out=bass.AP(gstg[:, :].tensor, gstg[:, :].offset, [[4096, 128], [128, 16], [1, 128]]),
               in_=bass.AP(gw.tensor, ic * 128, [[II, 128], [128 * II, 16], [1, 128]])),
           waits=[(s_ve, max(last_gcast, v_h2_all if ic == 0 else 0))], inc=s_gw[0], k=16)
        mk(kb, "sync",
           lambda e, ic=ic: e.dma_start(
               out=bass.AP(gstg[:, :].tensor, gstg[:, :].offset + 2048, [[4096, 128], [128, 16], [1, 128]]),
               in_=bass.AP(uw.tensor, ic * 128, [[II, 128], [128 * II, 16], [1, 128]])),
           inc=s_gw[0], k=16)
        GLD = (s_gw[0], 16 * n_gwl)
        v_f8 = mk(kb, "vector",
                  lambda e: e.tensor_copy(g8[:, :], gstg[:, :]),
                  waits=[GLD, (s_ac, last_abf)], inc=s_ve)
        last_gcast = v_f8
        a_bf = mk(kb, "scalar",
                  lambda e: e.activation(out=gwb[:, :], in_=g8[:, :], func=AF.Copy),
                  waits=[(s_ve, v_f8), (s_pe, last_gwb_pe), (s_sc, 48)], inc=s_ac)
        last_abf = a_bf
        for th in range(2):
            for eci in range(ECH):
                mk(kb, "tensor",
                   lambda e, gl=gl, th=th, eci=eci: e.matmul(
                       psGU[gl][th][:, :],
                       gwb[:, eci * 128:eci * 128 + 128],
                       h2_sb[:, eci * TPC + th * 512:eci * TPC + (th + 1) * 512],
                       start=(eci == 0), stop=(eci == ECH - 1)),
                   waits=([(s_ac, a_bf), (s_ve, last_fuse_ve[gl])]
                          if (th == 0 and eci == 0) else ()))
            for eci in range(ECH):
                mk(kb, "tensor",
                   lambda e, gl=gl, th=th, eci=eci: e.matmul(
                       psGU[gl][2 + th][:, :],
                       gwb[:, 2048 + eci * 128:2048 + eci * 128 + 128],
                       h2_sb[:, eci * TPC + th * 512:eci * TPC + (th + 1) * 512],
                       start=(eci == 0), stop=(eci == ECH - 1)),
                   inc=(s_pe if eci == ECH - 1 else None))
            pe_gu = s_pe.n
            if th == 1:
                last_gwb_pe = pe_gu
            a_si = mk(kb, "scalar",
                      lambda e, gl=gl, th=th, ic=ic: e.activation(
                          out=gact[:, th * 512:(th + 1) * 512],
                          in_=psGU[gl][th][:, :], func=AF.Silu,
                          scale=scl_sb[:, ic:ic + 1]),
                      waits=[(s_pe, pe_gu), (s_ve, last_gact_ve), (s_sc, 48)], inc=s_ac)
            mk(kb, "vector",
               lambda e, gl=gl, th=th: e.tensor_tensor(
                   ctmp[:, th * 512:(th + 1) * 512],
                   psGU[gl][2 + th][:, :],
                   gact[:, th * 512:(th + 1) * 512], op=OP.mult),
               waits=[(s_ac, a_si)], inc=s_ve)
            mk(kb, "vector",
               lambda e, th=th, ic=ic: e.tensor_scalar_mul(
                   sqtmp[:, th * 512:(th + 1) * 512],
                   ctmp[:, th * 512:(th + 1) * 512],
                   scl_sb[:, 48 + ic:48 + ic + 1]),
               inc=s_ve)
            v_fu = mk(kb, "vector",
                      lambda e, th=th, ic=ic: e.tensor_copy(
                          fuse[:, ic * TPC + th * 512:ic * TPC + (th + 1) * 512],
                          sqtmp[:, th * 512:(th + 1) * 512]),
                      inc=s_ve)
            last_gact_ve = v_fu
            if debug and ic == 0 and th == 0:
                mk(kb, "vector", lambda e: e.tensor_copy(ybuf[:, 0:512], psGU[0][2][:, :]),
                   waits=[(s_ve, v_fu)], inc=s_ve)
                mk(kb, "vector", lambda e: e.tensor_copy(ybuf[:, 512:1024], gact[:, 0:512]), inc=s_ve)
                v_c4 = s_ve.n
                mk(kb, "gpsimd", lambda e: e.dma_start(out=dbg["c4"][:, 0:1024], in_=ybuf[:, 0:TPC]),
                   waits=[(s_ve, v_c4)], inc=s_fin, k=16)
                mk(kb, "gpsimd", lambda e: e.dma_start(out=dbg["c4"][:, 1024:1536],
                                                       in_=ctmp[:, 0:512]), inc=s_fin, k=16)
                mk(kb, "gpsimd", lambda e: e.dma_start(out=dbg["c4"][:, 1536:2048],
                                                       in_=sqtmp[:, 0:512]), inc=s_fin, k=16)
                mk(kb, "gpsimd", lambda e: e.dma_start(out=dbg["g8"][:, :], in_=g8[:, 0:4096]),
                   inc=s_fin, k=16)
                mk(kb, "gpsimd", lambda e: e.dma_start(out=dbg["h2b"][:, :], in_=h2_sb[:, 0:2048]),
                   inc=s_fin, k=16)
                ic0_gate = [(s_fin, s_fin.n)]
            if debug and ic == 0 and th == 1:
                mk(kb, "vector", lambda e: e.engine_nop(), waits=ic0_gate, inc=s_ve)
        last_fuse_ve[gl] = s_ve.n
    fuse_done_ve = s_ve.n

    if debug:
        mk(kb, "gpsimd", lambda e: e.dma_start(out=dbg["fuse"][:, :], in_=fuse[:, :]),
           waits=[(s_ve, fuse_done_ve)], inc=s_fin, k=16)
        mk(kb, "gpsimd", lambda e: e.dma_start(out=dbg["h2"][:, :], in_=h2_sb[:, :]),
           waits=[(s_ve, v_h2_all)], inc=s_fin, k=16)

    kb.free("gact"); kb.free("gwb"); kb.free("g8"); kb.free("gstg")
    for i in range(1, -1, -1):
        for j in range(3, -1, -1):
            kb.free("psGU%d%d" % (i, j))
    kb.free("h2_sb")
    psY2 = [kb.ps("psY2_0", [128, 512]), kb.ps("psY2_1", [128, 512])]
    psSx_b = [kb.ps("psSx0b", [1, 512]), kb.ps("psSx1b", [1, 512])]
    psSxx_b = [kb.ps("psSxx0b", [1, 512]), kb.ps("psSxx1b", [1, 512])]
    psBB_b = kb.ps("psBBb", [128, 512])
    dstg = kb.sb("dstg", [128, II], F32)
    d8 = kb.sb("d8", [128, II], F8)
    dwb = kb.sb("dwb", [128, II], BF16)

    dbg_dump_gate = [(s_fin, s_fin.n)] if debug else []
    last_dcast = 0
    last_dbf = 0
    last_dwb_pe = 0
    last_psY2_ac = [0, 0]
    last_ctmp_pe = [0, 0]
    last_ybuf_done = [0, 0]
    n_dwl = 0
    for gc in range(ECH):
        gl = gc % 2
        n_dwl += 1
        mk(kb, "sync",
           lambda e, gc=gc: e.dma_start(
               out=bass.AP(dstg[:, :].tensor, dstg[:, :].offset, [[II, 128], [128, 48], [1, 128]]),
               in_=bass.AP(dw.tensor, gc * 128, [[E, 128], [128 * E, 48], [1, 128]])),
           waits=[(s_ve, max(last_dcast, fuse_done_ve if gc == 0 else 0))] +
                 (dbg_dump_gate if gc < 2 else []),
           inc=s_dw[0], k=16)
        DLD = (s_dw[0], 16 * n_dwl)
        v_f8 = mk(kb, "vector",
                  lambda e: e.tensor_copy(d8[:, :], dstg[:, :]),
                  waits=[DLD, (s_ac, last_dbf)], inc=s_ve)
        last_dcast = v_f8
        a_dbf = mk(kb, "scalar",
                   lambda e: e.activation(out=dwb[:, :], in_=d8[:, :], func=AF.Copy),
                   waits=[(s_ve, v_f8), (s_pe, last_dwb_pe)], inc=s_ac)
        last_dbf = a_dbf
        for th in range(2):
            for icc in range(48):
                mk(kb, "tensor",
                   lambda e, th=th, icc=icc: e.matmul(
                       psY2[th][:, :],
                       dwb[:, icc * 128:icc * 128 + 128],
                       fuse[:, icc * TPC + th * 512:icc * TPC + (th + 1) * 512],
                       start=(icc == 0), stop=(icc == 47)),
                   waits=([(s_ac, max(a_dbf, last_psY2_ac[th])), (s_ve, fuse_done_ve)]
                          if icc == 0 else ()),
                   inc=(s_pe if icc == 47 else None))
            pe_y2 = s_pe.n
            if th == 1:
                last_dwb_pe = pe_y2
            a_ev = mk(kb, "scalar",
                      lambda e, th=th, gl=gl, gc=gc: e.activation(
                          out=ybuf[:, th * 512:(th + 1) * 512],
                          in_=psY2[th][:, :], func=AF.Copy,
                          scale=scl_sb[:, 96 + gc:96 + gc + 1]),
                      waits=[(s_pe, pe_y2), (s_yd[0], 16 * n_yd[0]),
                             (s_yd[1], 16 * n_yd[1]),
                             (s_ve, max(last_ybuf_done))],
                      inc=s_ac)
            last_psY2_ac[th] = a_ev
        a_ev2 = last_psY2_ac[1]
        v_c = mk(kb, "vector",
                 lambda e, gl=gl: e.tensor_copy(ctmp[:, 0:TPC],
                                                ybuf[:, 0:TPC]),
                 waits=[(s_ac, a_ev2), (s_pe, max(last_ctmp_pe))], inc=s_ve)
        last_ybuf_done[gl] = v_c
        a_sq2 = mk(kb, "scalar",
                   lambda e, gl=gl: e.activation(out=sqtmp[:, 0:TPC],
                                                 in_=ybuf[:, 0:TPC],
                                                 func=AF.Square),
                   waits=[(s_ve, v_c)], inc=s_ac)
        for th in range(2):
            mk(kb, "tensor",
               lambda e, th=th, ps=psSx_b[th]: e.matmul(
                   ps[:, :], ones_fr[:, :],
                   ctmp[:, th * 512:(th + 1) * 512],
                   start=True, stop=True),
               waits=[(s_ve, v_c), (s_ac, a_sq2)])
            mk(kb, "tensor",
               lambda e, th=th, ps=psSxx_b[th]: e.matmul(
                   ps[:, :], ones_fr[:, :],
                   sqtmp[:, th * 512:(th + 1) * 512],
                   start=True, stop=True),
               inc=(s_pe if th == 1 else None))
        pe_st = s_pe.n
        for th in range(2):
            op0 = "copy" if gc == 0 else "add"
            if gc == 0:
                mk(kb, "vector", lambda e, th=th, ps=psSx_b[th]: e.tensor_copy(
                    runrows[0:1, th * 512:(th + 1) * 512], ps[:, :]),
                   waits=[(s_pe, pe_st)] if th == 0 else (), inc=s_ve)
                mk(kb, "vector", lambda e, th=th, ps=psSxx_b[th]: e.tensor_copy(
                    runrows[0:1, 1024 + th * 512:1024 + (th + 1) * 512], ps[:, :]),
                   inc=s_ve)
            else:
                mk(kb, "vector", lambda e, th=th, ps=psSx_b[th]: e.tensor_tensor(
                    runrows[0:1, th * 512:(th + 1) * 512],
                    runrows[0:1, th * 512:(th + 1) * 512], ps[:, :], op=OP.add),
                   waits=[(s_pe, pe_st)] if th == 0 else (), inc=s_ve)
                mk(kb, "vector", lambda e, th=th, ps=psSxx_b[th]: e.tensor_tensor(
                    runrows[0:1, 1024 + th * 512:1024 + (th + 1) * 512],
                    runrows[0:1, 1024 + th * 512:1024 + (th + 1) * 512], ps[:, :], op=OP.add),
                   inc=s_ve)
        last_stats_ve = s_ve.n
        last_ctmp_pe[gl] = s_pe.n
        n_yd[gl] += 1
        mk(kb, "gpsimd",
           lambda e, gc=gc, gl=gl: e.dma_start(
               out=bass.AP(y_dram.tensor, gc * 128 * TPC, [[TPC, 128], [1, TPC]]),
               in_=ybuf[:, 0:TPC]),
           waits=[(s_ac, a_ev2)], inc=s_yd[gl], k=16)
    pe_stats4 = s_pe.n
    last_stats_ve = emit_rows(psSx_b, psSxx_b, Rbc2, MRbc2, pe_stats4, psBB_b)

    kb.free("dwb"); kb.free("d8"); kb.free("dstg")
    ych_f = kb.sb("ych2", [128, 2 * TPC], F32)
    xch_f = kb.sb("xch2", [128, 2 * TPC], F32)
    tt1_f = kb.sb("tt1b", [128, TPC], F32)
    tt2_f = kb.sb("tt2b", [128, TPC], F32)
    y_done = [(s_yd[0], 16 * n_yd[0]), (s_yd[1], 16 * n_yd[1])]
    last_tt_use = [0, 0]
    last_out_st = [0, 0]
    for gc in range(ECH):
        gl = gc % 2
        n_ld[gl] += 2
        mk(kb, "gpsimd",
           lambda e, gc=gc, gl=gl: e.dma_start(
               out=ych_f[:, gl * TPC:(gl + 1) * TPC],
               in_=bass.AP(y_dram.tensor, gc * 128 * TPC, [[TPC, 128], [1, TPC]])),
           waits=(y_done if gc < 2 else []) + [(s_ve, last_tt_use[gl])],
           inc=s_ld[gl], k=16)
        mk(kb, "gpsimd",
           lambda e, gc=gc, gl=gl: e.dma_start(
               out=xch_f[:, gl * TPC:(gl + 1) * TPC],
               in_=bass.AP(st_dram.tensor, gc * 128 * TPC, [[TPC, 128], [1, TPC]])),
           inc=s_ld[gl], k=16)
        LD = (s_ld[gl], 16 * n_ld[gl])
        mk(kb, "vector",
           lambda e, gl=gl: e.tensor_tensor(tt1_f[:, :], ych_f[:, gl * TPC:(gl + 1) * TPC],
                                            Rbc2[:, :], op=OP.mult),
           waits=[LD, (s_ve, last_stats_ve)], inc=s_ve)
        mk(kb, "vector", lambda e: e.tensor_tensor(tt2_f[:, :], tt1_f[:, :], MRbc2[:, :],
                                                   op=OP.subtract), inc=s_ve)
        v_o = mk(kb, "vector",
                 lambda e, gl=gl: e.tensor_tensor(ybuf[:, 0:TPC], tt2_f[:, :],
                                                  xch_f[:, gl * TPC:(gl + 1) * TPC], op=OP.add),
                 waits=[(s_fin, max(last_out_st)), (s_yd[0], 16 * n_yd[0]),
                        (s_yd[1], 16 * n_yd[1])], inc=s_ve)
        last_tt_use[gl] = v_o
        mk(kb, "gpsimd",
           lambda e, gc=gc, gl=gl: e.dma_start(
               out=bass.AP(outT.tensor, gc * 128 * TPC, [[TPC, 128], [1, TPC]]),
               in_=ybuf[:, 0:TPC]),
           waits=[(s_ve, v_o)], inc=s_fin, k=16)
        last_out_st[gl] = s_fin.n

    if debug:
        mk(kb, "gpsimd", lambda e: e.dma_start(out=dbg["qT"][:, :], in_=q_dram[:, :]),
           waits=[(s_ve, s_ve.n)] + all_dma_done, inc=s_fin, k=16)
        mk(kb, "gpsimd", lambda e: e.dma_start(out=dbg["kT"][:, :], in_=k_dram[:, :]), inc=s_fin, k=16)
        mk(kb, "gpsimd", lambda e: e.dma_start(out=dbg["bnd"][:, :], in_=bnd_sb[:, :]),
           waits=[(s_ve, v_bnd)], inc=s_fin, k=16)
        mk(kb, "gpsimd", lambda e: e.dma_start(out=dbg["v"][:, :], in_=v_dram[:, :]), inc=s_fin, k=16)
        for i4 in range(HPC):
            mk(kb, "gpsimd", lambda e, i4=i4: e.dma_start(
                out=dbg["oT"][i4 * 128:(i4 + 1) * 128, :], in_=og_srcs[i4][:, :]),
               inc=s_fin, k=16)
        mk(kb, "gpsimd", lambda e: e.dma_start(out=dbg["stream"][:, :], in_=st_dram[:, :]),
           inc=s_fin, k=16)
        mk(kb, "gpsimd", lambda e: e.dma_start(out=dbg["yo"][:, :], in_=yo_dram[:, :]),
           inc=s_fin, k=16)
        mk(kb, "gpsimd", lambda e: e.dma_start(out=dbg["ogd"][:, :], in_=og_dsts[0][:, :]),
           inc=s_fin, k=16)
        mk(kb, "gpsimd", lambda e: e.dma_start(out=dbg["y"][:, :], in_=y_dram[:, :]),
           inc=s_fin, k=16)

    def fin(engine, n=s_fin.n):
        engine.wait_ge(s_fin.h, n)
    kb.ops["gpsimd"].append(fin)

    kb.emit_all()
    return nc


def kernel(**inputs):
    maps = prep_in_maps(inputs)
    nc = build_nc(debug=False)
    res = run_bass_kernel_spmd(nc, maps, core_ids=list(range(8)))
    out = np.zeros((B_, T, E), np.float32)
    for c in range(8):
        b, g = divmod(c, 4)
        out[b, g * TPC:(g + 1) * TPC, :] = np.asarray(res.results[c]["outT"], np.float32).T
    return out


# version 02 (final)
# speedup vs baseline: 1.0153x; 1.0153x over previous
import numpy as np
import ml_dtypes
from contextlib import ExitStack

import concourse.bass as bass
import concourse.mybir as mybir
from concourse.bass_utils import run_bass_kernel_spmd

F32 = mybir.dt.float32
BF16 = mybir.dt.bfloat16
F8 = mybir.dt.float8e4
F32R = mybir.dt.float32r
AF = mybir.ActivationFunctionType
OP = mybir.AluOpType
AX = mybir.AxisListType

B_, T, E, H, D, II = 2, 4096, 2048, 16, 128, 6144
HPC = 4
TPC = 1024
ECH = E // 128
NCH = 8
TCH = T // NCH


def _f32(a):
    return np.ascontiguousarray(np.asarray(a, dtype=np.float32))

def _bf16(a):
    return np.ascontiguousarray(np.asarray(a).astype(ml_dtypes.bfloat16))

def host_constants():
    inv = (1.0 / (1e6 ** (np.arange(0, D, 2, dtype=np.float32) / np.float32(D)))).astype(np.float32)
    f = (np.arange(T, dtype=np.float32)[:, None] * inv[None, :]).astype(np.float32).astype(np.float64)
    cos = np.concatenate([np.cos(f), np.cos(f)], -1)
    sin = np.concatenate([np.sin(f), np.sin(f)], -1)
    s = 1.0 / np.float64(np.float32(np.sqrt(np.float32(D), dtype=np.float32)))
    cosq = _f32((cos * s).T); sinq = _f32((sin * s).T)
    cosk = _f32(cos.T); sink = _f32(sin.T)
    Rm = np.zeros((D, D), np.float32)
    for i in range(64):
        Rm[i, i + 64] = -1.0
    for i in range(64, 128):
        Rm[i, i - 64] = 1.0
    RT = _bf16(Rm.T)
    r = np.arange(128)[:, None]; q = np.arange(512)[None, :]
    tri = np.zeros((128, 4 * 512), np.float32)
    for j in range(4):
        tri[:, j * 512:(j + 1) * 512] = (128 * j + r <= q)
    tri = _bf16(tri)
    cst = np.zeros((128, 8), np.float32)
    cst[:, 0] = 1e-5; cst[:, 1] = 1.0 / E; cst[:, 2] = -1.0; cst[:, 3] = 1.0
    return dict(cosq=cosq, sinq=sinq, cosk=cosk, sink=sink, RT=RT, tri=tri, cst=cst)

def prep_in_maps(inputs):
    cst = host_constants()
    x = np.asarray(inputs["L0_attn_x"], np.float32)
    xT = [np.ascontiguousarray(x[b].T) for b in range(B_)]
    wqT = _f32(np.asarray(inputs["L0_attn_wq"]).T)
    wkT = _f32(np.asarray(inputs["L0_attn_wk"]).T)
    wvT = _f32(np.asarray(inputs["L0_attn_wv"]).T)
    woT = _f32(np.asarray(inputs["L0_attn_wo"]).T)
    perm = np.concatenate([np.arange((4 * (b % 4) + b // 4) * 128,
                                     (4 * (b % 4) + b // 4) * 128 + 128)
                           for b in range(16)])
    woTp = np.ascontiguousarray(woT[perm, :])
    gw = _f32(inputs["L0_ffn_gate_w"]); uw = _f32(inputs["L0_ffn_up_w"])
    dw = _f32(inputs["L0_ffn_down_w"])
    gsT = _f32(np.asarray(inputs["L0_ffn_gate_scale"]).reshape(II // 128, 128).T)
    usT = _f32(np.asarray(inputs["L0_ffn_up_scale"]).reshape(II // 128, 128).T)
    dsT = _f32(np.asarray(inputs["L0_ffn_down_scale"]).reshape(E // 128, 128).T)
    maps = []
    for c in range(8):
        b, g = divmod(c, 4)
        m = dict(cst)
        m["xT"] = xT[b]
        m["wqT"] = np.ascontiguousarray(wqT[:, g * 512:(g + 1) * 512])
        m["wkT"] = np.ascontiguousarray(wkT[:, g * 512:(g + 1) * 512])
        m["wvT"] = np.ascontiguousarray(wvT[:, g * 512:(g + 1) * 512])
        m["woTp"] = woTp
        m["xownT"] = np.ascontiguousarray(xT[b][:, g * TPC:(g + 1) * TPC])
        m["tokb"] = np.array([[g * TPC]], np.int32)
        m["gw"] = gw; m["uw"] = uw; m["dw"] = dw
        m["gsT"] = gsT; m["usT"] = usT; m["dsT"] = dsT
        maps.append(m)
    return maps


ENGINES = ("sync", "tensor", "vector", "scalar", "gpsimd")

class Sem:
    __slots__ = ("h", "n")
    def __init__(self, h):
        self.h = h; self.n = 0

class KB:
    def __init__(self, nc):
        self.nc = nc
        self.es = ExitStack()
        self.ops = {e: [] for e in ENGINES}
        self._guards = {}
    def sem(self, name):
        return Sem(self.es.enter_context(self.nc.semaphore(name)))
    def sb(self, name, shape, dt):
        g = self.nc.sbuf_tensor(name, shape, dt)
        h = g.__enter__(); self._guards[name] = g
        return h
    def free(self, name):
        self._guards.pop(name).__exit__(None, None, None)
    def ps(self, name, shape, dt=F32):
        g = self.nc.psum_tensor(name, shape, dt)
        h = g.__enter__(); self._guards[name] = g
        return h
    def emit_all(self):
        with self.nc.Block() as blk:
            for e in ENGINES:
                getattr(blk, e)(lambda eng, e=e: [f(eng) for f in self.ops[e]])
        for g in reversed(list(self._guards)):
            self.free(g)
        self.es.close()

def mk(kb, eng, fn, waits=(), inc=None, k=1):
    ws = tuple((s.h, int(n)) for (s, n) in waits)
    ih = None
    if inc is not None:
        inc.n += k; ih = inc.h
    def run(engine, fn=fn, ws=ws, ih=ih, k=k):
        for (h, n) in ws:
            engine.wait_ge(h, n)
        r = fn(engine)
        if ih is not None:
            r.then_inc(ih, k)
    kb.ops[eng].append(run)
    return inc.n if inc is not None else None

def ap3(t2, outer_step, outer_n, width, col0=0, part0=0, parts=128):
    a = t2[:, :] if hasattr(t2, "__getitem__") else t2
    row = a.ap[0][0]
    off = a.offset + col0 + part0 * row
    return bass.AP(a.tensor, off, [[row, parts], [outer_step, outer_n], [1, width]])


def build_nc(debug=False):
    nc = bass.Bass("TRN2", target_bir_lowering=False, dynamic_dma_scratch_size=16384)
    kb = KB(nc)

    xT = nc.dram_tensor("xT", [E, T], F32, kind="ExternalInput").ap()
    wqT = nc.dram_tensor("wqT", [E, 512], F32, kind="ExternalInput").ap()
    wkT = nc.dram_tensor("wkT", [E, 512], F32, kind="ExternalInput").ap()
    wvT = nc.dram_tensor("wvT", [E, 512], F32, kind="ExternalInput").ap()
    gw = nc.dram_tensor("gw", [E, II], F32, kind="ExternalInput").ap()
    uw = nc.dram_tensor("uw", [E, II], F32, kind="ExternalInput").ap()
    dw = nc.dram_tensor("dw", [II, E], F32, kind="ExternalInput").ap()
    gsT = nc.dram_tensor("gsT", [128, II // 128], F32, kind="ExternalInput").ap()
    usT = nc.dram_tensor("usT", [128, II // 128], F32, kind="ExternalInput").ap()
    dsT = nc.dram_tensor("dsT", [128, E // 128], F32, kind="ExternalInput").ap()
    cosq = nc.dram_tensor("cosq", [128, T], F32, kind="ExternalInput").ap()
    sinq = nc.dram_tensor("sinq", [128, T], F32, kind="ExternalInput").ap()
    cosk = nc.dram_tensor("cosk", [128, T], F32, kind="ExternalInput").ap()
    sink = nc.dram_tensor("sink", [128, T], F32, kind="ExternalInput").ap()
    RT = nc.dram_tensor("RT", [128, 128], BF16, kind="ExternalInput").ap()
    tri = nc.dram_tensor("tri", [128, 4 * 512], BF16, kind="ExternalInput").ap()
    cstd = nc.dram_tensor("cst", [128, 8], F32, kind="ExternalInput").ap()

    v_dram = nc.dram_tensor("v_dram", [T, 512], BF16).ap()
    xownT = nc.dram_tensor("xownT", [E, TPC], F32, kind="ExternalInput").ap()
    woTp = nc.dram_tensor("woTp", [E, E], F32, kind="ExternalInput").ap()
    tokb = nc.dram_tensor("tokb", [1, 1], mybir.dt.int32, kind="ExternalInput").ap()
    og_srcs = [nc.dram_tensor("og_src%d" % i, [128, T], BF16).ap() for i in range(HPC)]
    og_dsts = [nc.dram_tensor("og_dst%d" % i, [HPC * 128, T], BF16).ap() for i in range(HPC)]
    yo_dram = nc.dram_tensor("yo_dram", [E, TPC], F32).ap()
    st_dram = nc.dram_tensor("st_dram", [E, TPC], F32).ap()
    y_dram = nc.dram_tensor("y_dram", [E, TPC], F32).ap()
    q_dram = nc.dram_tensor("q_dram", [128, HPC * T], BF16).ap()
    k_dram = nc.dram_tensor("k_dram", [128, HPC * T], BF16).ap()
    outT = nc.dram_tensor("outT", [E, TPC], F32, kind="ExternalOutput").ap()
    dbg = {}
    if debug:
        dbg["qT"] = nc.dram_tensor("dbg_qT", [128, HPC * T], BF16, kind="ExternalOutput").ap()
        dbg["kT"] = nc.dram_tensor("dbg_kT", [128, HPC * T], BF16, kind="ExternalOutput").ap()
        dbg["bnd"] = nc.dram_tensor("dbg_bnd", [128, HPC], F32, kind="ExternalOutput").ap()
        dbg["v"] = nc.dram_tensor("dbg_v", [T, 512], BF16, kind="ExternalOutput").ap()

        dbg["oT"] = nc.dram_tensor("dbg_oT", [HPC * 128, T], BF16, kind="ExternalOutput").ap()
        dbg["stream"] = nc.dram_tensor("dbg_stream", [E, TPC], F32, kind="ExternalOutput").ap()
        dbg["yo"] = nc.dram_tensor("dbg_yo", [E, TPC], F32, kind="ExternalOutput").ap()
        dbg["ogd"] = nc.dram_tensor("dbg_ogd", [HPC * 128, T], BF16, kind="ExternalOutput").ap()
        dbg["sx"] = nc.dram_tensor("dbg_sx", [1, 2 * TPC], F32, kind="ExternalOutput").ap()
        dbg["rows"] = nc.dram_tensor("dbg_rows", [1, 2048], F32, kind="ExternalOutput").ap()
        dbg["fuse"] = nc.dram_tensor("dbg_fuse", [128, 48 * TPC], BF16, kind="ExternalOutput").ap()
        dbg["h2"] = nc.dram_tensor("dbg_h2", [128, ECH * TPC], BF16, kind="ExternalOutput").ap()
        dbg["c3"] = nc.dram_tensor("dbg_c3", [128, 3 * TPC], F32, kind="ExternalOutput").ap()
        dbg["c4"] = nc.dram_tensor("dbg_c4", [128, 4 * 512], F32, kind="ExternalOutput").ap()
        dbg["g8"] = nc.dram_tensor("dbg_g8", [128, 4096], F8, kind="ExternalOutput").ap()
        dbg["h2b"] = nc.dram_tensor("dbg_h2b", [128, 2048], BF16, kind="ExternalOutput").ap()
        dbg["r2"] = nc.dram_tensor("dbg_r2", [128, 2 * TPC], F32, kind="ExternalOutput").ap()
        dbg["y"] = nc.dram_tensor("dbg_y", [E, TPC], F32, kind="ExternalOutput").ap()

    s_pe = kb.sem("s_pe")
    s_ve = kb.sem("s_ve")
    s_ac = kb.sem("s_ac")
    s_pre = kb.sem("s_pre")
    s_w = [kb.sem("s_w%d" % i) for i in range(3)]
    s_x = [kb.sem("s_x%d" % i) for i in range(2)]
    s_r = [kb.sem("s_r%d" % i) for i in range(2)]
    s_vs = [kb.sem("s_vs%d" % i) for i in range(2)]
    s_qs = [kb.sem("s_qs%d" % i) for i in range(2)]
    s_fin = kb.sem("s_fin")

    cst_sb = kb.sb("cst_sb", [128, 8], F32)
    ones_bf = kb.sb("ones_bf", [128, 1], BF16)
    ones_f = kb.sb("ones_f", [1, 128], F32)
    bnd_sb = kb.sb("bnd_sb", [128, HPC], F32)
    w_sb = kb.sb("w_sb", [128, 3 * ECH * 512], BF16)
    RT_sb = kb.sb("RT_sb", [128, 128], BF16)
    rows = kb.sb("rows", [1, 512 * 4], F32)
    nmax = kb.sb("nmax", [1, 64], F32)
    RM = dict(m=rows[0:1, 0:512], e=rows[0:1, 512:1024], v=rows[0:1, 512:1024],
              r=rows[0:1, 1024:1536], mr=rows[0:1, 1536:2048], t=rows[0:1, 1024:1536])

    for col, val in ((0, 1e-5), (1, 1.0 / E), (2, -1.0), (3, 1.0), (4, 0.0)):
        nc.const_aps.register(F32, float(val), cst_sb[:, col:col + 1])

    mk(kb, "sync", lambda e: e.dma_start(out=cst_sb[:, :], in_=cstd[:, :]), inc=s_pre, k=16)
    mk(kb, "sync", lambda e: e.dma_start(out=RT_sb[:, :], in_=RT[:, :]), inc=s_pre, k=16)
    PRE = (s_pre, 32)
    mk(kb, "vector", lambda e: e.memset(ones_bf[:, :], 1.0), inc=s_ve)
    mk(kb, "vector", lambda e: e.memset(ones_f[:, :], 1.0), inc=s_ve)

    wstage = kb.sb("wstage", [128, ECH * 512], F32)
    prev_cast = 0
    for p, wsrc in enumerate((wqT, wkT, wvT)):
        mk(kb, "sync",
           lambda e, wsrc=wsrc: e.dma_start(
               out=ap3(wstage, 512, ECH, 512),
               in_=bass.AP(wsrc.tensor, 0, [[512, 128], [128 * 512, ECH], [1, 512]])),
           waits=[(s_ve, prev_cast)], inc=s_w[p], k=16)
        prev_cast = mk(kb, "vector",
                       lambda e, p=p: e.tensor_copy(w_sb[:, p * ECH * 512:(p + 1) * ECH * 512], wstage[:, :]),
                       waits=[(s_w[p], 16)], inc=s_ve)
    v_wcast = s_ve.n
    kb.free("wstage")

    xbuf = [kb.sb("xbuf0", [128, ECH * TCH], F32), kb.sb("xbuf1", [128, ECH * TCH], F32)]
    ropeb = [kb.sb("ropeb0", [128, 4 * TCH], F32), kb.sb("ropeb1", [128, 4 * TCH], F32)]
    bufA = kb.sb("bufA", [128, ECH * TCH], BF16)
    bufB = kb.sb("bufB", [128, ECH * TCH], BF16)
    qraw = kb.sb("qraw", [128, HPC * TCH], BF16)
    qstg = kb.sb("qstg", [128, HPC * 512], BF16)
    qsq = kb.sb("qsq", [128, HPC * TCH], BF16)
    vst = kb.sb("vst", [128, 4 * 512], BF16)
    rtmp = kb.sb("rtmp", [128, 2 * TCH], BF16)

    psStatA = kb.ps("psStatA", [1, 512])
    psStatB = kb.ps("psStatB", [1, 512])
    psAux = [kb.ps("psAux0", [128, 512]), kb.ps("psAux1", [128, 512])]
    psProj = kb.ps("psProj", [128, 4 * 512])
    Rbc = kb.sb("Rbc", [128, 512], F32)
    MRbc = kb.sb("MRbc", [128, 512], F32)

    last_x_use = [[(s_ve, v_wcast)], [(s_ve, v_wcast)]]
    last_rope_use = [v_wcast, v_wcast]
    last_stat_read = v_wcast
    last_norm_read = 0
    last_aux_read = [v_wcast, v_wcast]
    last_proj_evac = 0
    n_vst = [0, 0]
    n_x = [0, 0]
    n_r = [0, 0]
    n_qs = [0, 0]
    last_qsq_pe = 0
    last_qsq_a = 0
    last_bufA_pe = 0
    last_bufB_pe = 0

    for ci in range(NCH):
        T0 = ci * TCH
        par = ci % 2
        xb, rb = xbuf[par], ropeb[par]
        n_x[par] += 1
        mk(kb, "sync",
           lambda e, T0=T0, xb=xb: e.dma_start(
               out=ap3(xb, TCH, ECH, TCH),
               in_=bass.AP(xT.tensor, T0, [[T, 128], [128 * T, ECH], [1, TCH]])),
           waits=last_x_use[par], inc=s_x[par], k=16)
        LDX = (s_x[par], 16 * n_x[par])
        n_r[par] += 4
        for j, tab in enumerate((cosq, sinq, cosk, sink)):
            mk(kb, "gpsimd",
               lambda e, tab=tab, j=j, T0=T0, rb=rb: e.dma_start(
                   out=rb[:, j * TCH:(j + 1) * TCH], in_=tab[:, T0:T0 + TCH]),
               waits=[(s_ve, last_rope_use[par])] if j == 0 else (),
               inc=s_r[par], k=16)
        LDR = (s_r[par], 16 * n_r[par])
        a_sq = mk(kb, "scalar",
                  lambda e, xb=xb: e.activation(out=bufA[:, :], in_=xb[:, :], func=AF.Square),
                  waits=[LDX, (s_ve, last_stat_read), (s_pe, last_bufA_pe)], inc=s_ac)
        v_cp = mk(kb, "vector", lambda e, xb=xb: e.tensor_copy(bufB[:, :], xb[:, :]),
                  waits=[LDX, (s_ve, last_stat_read), (s_ac, last_qsq_a),
                         (s_pe, last_bufB_pe)], inc=s_ve)
        pe_w = [(s_ac, a_sq), (s_ve, max(v_cp, last_norm_read))]
        for eci in range(ECH):
            mk(kb, "tensor",
               lambda e, eci=eci: e.matmul(psStatA[:, :], ones_bf[:, :],
                                           bufB[:, eci * TCH:(eci + 1) * TCH],
                                           start=(eci == 0), stop=(eci == ECH - 1)),
               waits=pe_w if eci == 0 else ())
        for eci in range(ECH):
            mk(kb, "tensor",
               lambda e, eci=eci: e.matmul(psStatB[:, :], ones_bf[:, :],
                                           bufA[:, eci * TCH:(eci + 1) * TCH],
                                           start=(eci == 0), stop=(eci == ECH - 1)),
               inc=(s_pe if eci == ECH - 1 else None))
        pe_stats = s_pe.n
        last_bufA_pe = pe_stats
        mk(kb, "vector", lambda e: e.tensor_scalar_mul(RM["m"], psStatA[:, :], 1.0 / E),
           waits=[(s_pe, pe_stats), PRE], inc=s_ve)
        mk(kb, "vector", lambda e: e.tensor_scalar_mul(RM["e"], psStatB[:, :], 1.0 / E), inc=s_ve)
        mk(kb, "vector", lambda e: e.tensor_tensor(RM["t"], RM["m"], RM["m"], op=OP.mult), inc=s_ve)
        mk(kb, "vector", lambda e: e.tensor_tensor(RM["v"], RM["e"], RM["t"], op=OP.subtract), inc=s_ve)
        v_var = mk(kb, "vector", lambda e: e.tensor_scalar_max(RM["v"], RM["v"], 0.0), inc=s_ve)
        last_stat_read = v_var
        a_sqrt = mk(kb, "scalar",
                    lambda e: e.activation(out=RM["v"], in_=RM["v"], func=AF.Sqrt,
                                           bias=cst_sb[0:1, 0:1]),
                    waits=[(s_ve, v_var), PRE], inc=s_ac)
        mk(kb, "vector", lambda e: e.reciprocal(RM["r"], RM["v"]),
           waits=[(s_ac, a_sqrt)], inc=s_ve)
        v_rows = mk(kb, "vector", lambda e: e.tensor_tensor(RM["mr"], RM["m"], RM["r"], op=OP.mult), inc=s_ve)
        mk(kb, "tensor", lambda e: e.matmul(psAux[0][:, :], ones_f[:, :], RM["r"], start=True, stop=True),
           waits=[(s_ve, max(v_rows, last_aux_read[0]))])
        pe_bc = mk(kb, "tensor", lambda e: e.matmul(psAux[1][:, :], ones_f[:, :], RM["mr"], start=True, stop=True),
                   waits=[(s_ve, last_aux_read[1])], inc=s_pe)
        mk(kb, "vector", lambda e: e.tensor_copy(Rbc[:, :], psAux[0][:, :]),
           waits=[(s_pe, pe_bc)], inc=s_ve)
        v_bc = mk(kb, "vector", lambda e: e.tensor_copy(MRbc[:, :], psAux[1][:, :]), inc=s_ve)
        last_aux_read[0] = last_aux_read[1] = v_bc
        mk(kb, "vector",
           lambda e, xb=xb: e.tensor_tensor(ap3(bufA, TCH, ECH, TCH), ap3(xb, TCH, ECH, TCH),
                                            bass.AP(Rbc[:, :].tensor, Rbc[:, :].offset,
                                                    [[512, 128], [0, ECH], [1, TCH]]),
                                            op=OP.mult), inc=s_ve)
        v_h = mk(kb, "vector",
                 lambda e: e.tensor_tensor(ap3(bufB, TCH, ECH, TCH), ap3(bufA, TCH, ECH, TCH),
                                           bass.AP(MRbc[:, :].tensor, MRbc[:, :].offset,
                                                   [[512, 128], [0, ECH], [1, TCH]]),
                                           op=OP.subtract), inc=s_ve)
        last_x_use[par] = [(s_ve, v_h), (s_ac, a_sq)]

        for pk, (ddst, ctab_off, nmcell) in enumerate(((q_dram, 0, 0), (k_dram, 2, 4))):
            for f in range(4):
                for eci in range(ECH):
                    mk(kb, "tensor",
                       lambda e, pk=pk, f=f, eci=eci: e.matmul(
                           psProj[:, f * 512:(f + 1) * 512],
                           w_sb[:, (pk * ECH + eci) * 512 + f * 128:(pk * ECH + eci) * 512 + f * 128 + 128],
                           bufB[:, eci * TCH:(eci + 1) * TCH],
                           start=(eci == 0), stop=(eci == ECH - 1)),
                       waits=([(s_ve, v_h), (s_ac, last_proj_evac)] if (f == 0 and eci == 0) else ()),
                       inc=(s_pe if eci == ECH - 1 and f == 3 else None))
            pe_proj = s_pe.n
            a_ev = mk(kb, "scalar", lambda e: e.copy(qraw[:, :], psProj[:, :]),
                      waits=[(s_pe, pe_proj), (s_ve, s_ve.n)], inc=s_ac)
            last_proj_evac = a_ev
            for f in range(4):
                aux = f % 2
                pe_rot = mk(kb, "tensor",
                            lambda e, f=f, aux=aux: e.matmul(psAux[aux][:, :], RT_sb[:, :],
                                                             qraw[:, f * 512:(f + 1) * 512],
                                                             start=True, stop=True),
                            waits=[(s_ac, a_ev), (s_ve, last_aux_read[aux])] + ([PRE] if ci == 0 else []),
                            inc=s_pe)
                mk(kb, "vector",
                   lambda e, aux=aux, rb=rb, o=ctab_off: e.tensor_tensor(
                       rtmp[:, 0:TCH], psAux[aux][:, :], rb[:, (o + 1) * TCH:(o + 2) * TCH], op=OP.mult),
                   waits=[(s_pe, pe_rot), LDR] + ([(s_ac, last_qsq_a), (s_qs[0], 16 * n_qs[0]), (s_qs[1], 16 * n_qs[1])] if f == 0 else []),
                   inc=s_ve)
                mk(kb, "vector",
                   lambda e, f=f, rb=rb, o=ctab_off: e.tensor_tensor(
                       rtmp[:, TCH:2 * TCH], qraw[:, f * 512:(f + 1) * 512],
                       rb[:, o * TCH:(o + 1) * TCH], op=OP.mult), inc=s_ve)
                v_rope = mk(kb, "vector",
                            lambda e, f=f: e.tensor_tensor(
                                qstg[:, f * 512:(f + 1) * 512],
                                rtmp[:, 0:TCH], rtmp[:, TCH:2 * TCH], op=OP.add), inc=s_ve)
                last_aux_read[aux] = v_rope
            v_rope_all = s_ve.n
            n_qs[pk] += 4
            for f4 in range(4):
                mk(kb, "gpsimd",
                   lambda e, ddst=ddst, T0=T0, f4=f4: e.dma_start(
                       out=bass.AP(ddst.tensor, f4 * T + T0, [[HPC * T, 128], [1, 512]]),
                       in_=qstg[:, f4 * 512:(f4 + 1) * 512]),
                   waits=[(s_ve, v_rope_all)] if f4 == 0 else (), inc=s_qs[pk], k=16)
            last_qsq_a = mk(kb, "scalar",
                            lambda e: e.activation(out=qsq[:, :], in_=qstg[:, :], func=AF.Square),
                            waits=[(s_ve, v_rope_all), (s_pe, last_qsq_pe)], inc=s_ac)
            for f in range(4):
                stat = psStatA if pk == 0 else psStatB
                pe_n = mk(kb, "tensor",
                          lambda e, f=f, stat=stat: e.matmul(stat[:, :], ones_bf[:, :],
                                                             qsq[:, f * 512:(f + 1) * 512],
                                                             start=True, stop=True),
                          waits=[(s_ac, last_qsq_a), (s_ve, max(last_stat_read, last_norm_read))],
                          inc=s_pe)
                last_qsq_pe = pe_n
                v_red = mk(kb, "vector",
                           lambda e, stat=stat: e.reduce_max(nmax[0:1, 8:9], stat[:, :], axis=AX.X),
                           waits=[(s_pe, pe_n)], inc=s_ve)
                cell = nmcell + f
                if ci == 0:
                    v_n = mk(kb, "vector",
                             lambda e, cell=cell: e.tensor_copy(nmax[0:1, cell:cell + 1], nmax[0:1, 8:9]),
                             waits=[(s_ve, v_red)], inc=s_ve)
                else:
                    v_n = mk(kb, "vector",
                             lambda e, cell=cell: e.tensor_tensor(nmax[0:1, cell:cell + 1],
                                                                  nmax[0:1, cell:cell + 1],
                                                                  nmax[0:1, 8:9], op=OP.max),
                             waits=[(s_ve, v_red)], inc=s_ve)
                last_norm_read = v_n

        for ts in range(4):
            for eci in range(ECH):
                mk(kb, "tensor",
                   lambda e, ts=ts, eci=eci: e.matmul(
                       psProj[:, ts * 512:(ts + 1) * 512],
                       bufB[:, eci * TCH + ts * 128:eci * TCH + ts * 128 + 128],
                       w_sb[:, (2 * ECH + eci) * 512:(2 * ECH + eci) * 512 + 512],
                       start=(eci == 0), stop=(eci == ECH - 1)),
                   waits=[(s_ac, last_proj_evac)] if (ts == 0 and eci == 0) else (),
                   inc=(s_pe if eci == ECH - 1 else None))
            pe_v = s_pe.n
            a_vev = mk(kb, "scalar",
                       lambda e, ts=ts: e.copy(vst[:, ts * 512:(ts + 1) * 512],
                                               psProj[:, ts * 512:(ts + 1) * 512]),
                       waits=[(s_pe, pe_v), (s_vs[1 - par], 64 * n_vst[1 - par])], inc=s_ac)
            if ts == 3:
                last_proj_evac = a_vev
            mk(kb, "gpsimd",
               lambda e, ts=ts, T0=T0: e.dma_start(
                   out=v_dram[T0 + ts * 128:T0 + ts * 128 + 128, :],
                   in_=vst[:, ts * 512:(ts + 1) * 512]),
               waits=[(s_ac, a_vev)], inc=s_vs[par], k=16)
        n_vst[par] += 1
        last_bufB_pe = s_pe.n

    mk(kb, "vector", lambda e: e.tensor_tensor(nmax[0:1, 16:20], nmax[0:1, 0:4],
                                               nmax[0:1, 4:8], op=OP.mult),
       waits=[(s_ve, s_ve.n)], inc=s_ve)
    a_bs = mk(kb, "scalar", lambda e: e.activation(out=nmax[0:1, 20:24],
                                                   in_=nmax[0:1, 16:20], func=AF.Sqrt),
              waits=[(s_ve, s_ve.n)], inc=s_ac)
    v_bn = mk(kb, "vector", lambda e: e.tensor_scalar_mul(nmax[0:1, 24:28],
                                                          nmax[0:1, 20:24], -1.0),
              waits=[(s_ac, a_bs)], inc=s_ve)
    pe_bb = mk(kb, "tensor", lambda e: e.matmul(psAux[0][:, 0:HPC], ones_f[:, :],
                                                nmax[0:1, 24:24 + HPC], start=True, stop=True),
               waits=[(s_ve, max(v_bn, last_aux_read[0]))], inc=s_pe)
    v_bnd = mk(kb, "vector", lambda e: e.tensor_copy(bnd_sb[:, :], psAux[0][:, 0:HPC]),
               waits=[(s_pe, pe_bb)], inc=s_ve)

    all_dma_done = [(s_qs[0], 16 * n_qs[0]), (s_qs[1], 16 * n_qs[1]),
                    (s_vs[0], 64 * n_vst[0]), (s_vs[1], 64 * n_vst[1])]

    for nm in ("MRbc", "Rbc", "psProj", "psAux1", "psAux0", "psStatB", "psStatA",
               "rtmp", "vst", "qsq", "qstg", "qraw", "bufB", "bufA",
               "ropeb1", "ropeb0", "xbuf1", "xbuf0", "nmax", "rows", "RT_sb", "w_sb"):
        kb.free(nm)

    tri_sb = kb.sb("tri_sb", [128, 4 * 512], BF16)
    s_tri = kb.sem("s_tri")
    qtile = kb.sb("qtile", [128, 2 * 512], BF16)
    kgrp = kb.sb("kgrp", [128, 2 * 512], BF16)
    vgrp = kb.sb("vgrp", [128, 2 * 512], BF16)
    p_sb = kb.sb("p_sb", [128, 3 * 512], BF16)
    drow = kb.sb("drow", [1, 1024], F32)
    Rcp = kb.sb("Rcp", [128, 512], F32)
    ost = kb.sb("ost", [128, 2 * 512], BF16)
    psS = [kb.ps("psS0", [128, 512]), kb.ps("psS1", [128, 512])]
    psO = [kb.ps("psO0", [128, 512]), kb.ps("psO1", [128, 512])]
    psDen = [kb.ps("psDen0", [1, 512]), kb.ps("psDen1", [1, 512])]
    psB = kb.ps("psB", [128, 512])

    s_qt = kb.sem("s_qt")
    s_kg = [kb.sem("s_kg0"), kb.sem("s_kg1")]
    s_vg = [kb.sem("s_vg0"), kb.sem("s_vg1")]
    s_ost = [kb.sem("s_ost0"), kb.sem("s_ost1")]

    BAR = [(s_pe, s_pe.n), (s_ve, s_ve.n), (s_ac, s_ac.n)] + all_dma_done + \
          [(s_x[0], 16 * n_x[0]), (s_x[1], 16 * n_x[1]),
           (s_r[0], 16 * n_r[0]), (s_r[1], 16 * n_r[1])]
    bar_used = {e: False for e in ENGINES}
    mk(kb, "sync", lambda e: e.dma_start(out=tri_sb[:, :], in_=tri[:, :]),
       waits=BAR, inc=s_tri, k=16)
    bar_used["sync"] = True
    def barw(eng, extra=()):
        w = list(extra)
        if not bar_used[eng]:
            w = BAR + w
            bar_used[eng] = True
        return w

    n_qld = 0
    n_kg = [0, 0]; n_vg = [0, 0]; n_ost = [0, 0]
    last_qtile_pe = [0, 0]
    last_kg_pe = [0, 0]
    last_vg_pe = [0, 0]
    last_psS_ac = [0, 0]
    last_p_pe = [0, 0, 0]
    last_den_ve = [0, 0]
    last_O_ve = [0, 0]
    last_psB_ve = 0
    qk_of_p = [0, 0, 0]

    for lh in range(HPC):
        for qt in range(8):
            NK = 4 * qt + 4
            qb = (lh * 8 + qt) % 2
            n_qld += 1
            mk(kb, "gpsimd",
               lambda e, lh=lh, qt=qt, qb=qb: e.dma_start(
                   out=qtile[:, qb * 512:(qb + 1) * 512],
                   in_=q_dram[:, lh * T + qt * 512:lh * T + qt * 512 + 512]),
               waits=barw("gpsimd", [(s_pe, last_qtile_pe[qb])]), inc=s_qt, k=16)
            QLD = (s_qt, 16 * n_qld)
            exp_last = 0
            for g in range(qt + 1):
                gl = g % 2
                n_kg[gl] += 1
                mk(kb, "sync",
                   lambda e, lh=lh, g=g, gl=gl: e.dma_start(
                       out=kgrp[:, gl * 512:(gl + 1) * 512],
                       in_=k_dram[:, lh * T + g * 512:lh * T + g * 512 + 512]),
                   waits=barw("sync", [(s_pe, last_kg_pe[gl])]), inc=s_kg[gl], k=16)
                KLD = (s_kg[gl], 16 * n_kg[gl])
                n_vg[gl] += 1
                mk(kb, "sync",
                   lambda e, lh=lh, g=g, gl=gl: e.dma_start(
                       out=bass.AP(vgrp[:, :].tensor, vgrp[:, :].offset + gl * 512,
                                   [[1024, 128], [128, 4], [1, 128]]),
                       in_=bass.AP(v_dram.tensor, g * 512 * 512 + lh * 128,
                                   [[512, 128], [128 * 512, 4], [1, 128]])),
                   waits=[(s_pe, last_vg_pe[gl])], inc=s_vg[gl], k=16)
                VLD = (s_vg[gl], 16 * n_vg[gl])
                for j in range(4):
                    kt = g * 4 + j
                    sb_ = kt % 2
                    ps_ = kt % 3
                    pe_qk = mk(kb, "tensor",
                               lambda e, gl=gl, j=j, qb=qb, sb_=sb_: e.matmul(
                                   psS[sb_][:, :],
                                   kgrp[:, gl * 512 + j * 128:gl * 512 + j * 128 + 128],
                                   qtile[:, qb * 512:(qb + 1) * 512],
                                   start=True, stop=True),
                               waits=barw("tensor", [KLD, QLD, (s_ac, last_psS_ac[sb_]),
                                                     (s_ve, last_O_ve[qt % 2] if kt == 0 else 0),
                                                     (s_ve, last_den_ve[qt % 2] if kt == 0 else 0)]),
                               inc=s_pe)
                    a_exp = mk(kb, "scalar",
                               lambda e, sb_=sb_, ps_=ps_, lh=lh: e.activation(
                                   out=p_sb[:, ps_ * 512:(ps_ + 1) * 512],
                                   in_=psS[sb_][:, :], func=AF.Exp,
                                   bias=bnd_sb[:, lh:lh + 1]),
                               waits=barw("scalar", [(s_pe, max(pe_qk, last_p_pe[ps_]))]),
                               inc=s_ac)
                    last_psS_ac[sb_] = a_exp
                    qk_of_p[ps_] = a_exp
                    diag = (kt >= 4 * qt)
                    p_ready = [(s_ac, a_exp)]
                    if diag:
                        jj = kt - 4 * qt
                        v_msk = mk(kb, "vector",
                                   lambda e, ps_=ps_, jj=jj: e.tensor_tensor(
                                       p_sb[:, ps_ * 512:(ps_ + 1) * 512],
                                       p_sb[:, ps_ * 512:(ps_ + 1) * 512],
                                       tri_sb[:, jj * 512:(jj + 1) * 512], op=OP.mult),
                                   waits=barw("vector", [(s_ac, a_exp), (s_tri, 16)]), inc=s_ve)
                        p_ready = [(s_ve, v_msk)]
                    mk(kb, "tensor",
                       lambda e, ps_=ps_, qt=qt, first=(kt == 0), lastk=(kt == NK - 1): e.matmul(
                           psDen[qt % 2][:, :], ones_bf[:, :],
                           p_sb[:, ps_ * 512:(ps_ + 1) * 512],
                           start=first, stop=lastk),
                       waits=p_ready)
                    pe_av = mk(kb, "tensor",
                               lambda e, gl=gl, j=j, ps_=ps_, qt=qt,
                                      first=(kt == 0), lastk=(kt == NK - 1): e.matmul(
                                   psO[qt % 2][:, :],
                                   bass.AP(vgrp[:, :].tensor, vgrp[:, :].offset + gl * 512 + j * 128,
                                           [[1024, 128], [1, 128]]),
                                   p_sb[:, ps_ * 512:(ps_ + 1) * 512],
                                   start=first, stop=lastk),
                               waits=[VLD], inc=s_pe)
                    last_p_pe[ps_] = pe_av
                    last_kg_pe[gl] = pe_av
                    last_vg_pe[gl] = pe_av
                    last_qtile_pe[qb] = pe_av
                    exp_last = pe_av
            v_den = mk(kb, "vector",
                       lambda e, qt=qt: e.tensor_copy(drow[0:1, 0:512], psDen[qt % 2][:, :]),
                       waits=barw("vector", [(s_pe, exp_last)]), inc=s_ve)
            last_den_ve[qt % 2] = v_den
            v_rcp = mk(kb, "vector",
                       lambda e: e.reciprocal(drow[0:1, 512:1024], drow[0:1, 0:512]),
                       waits=[(s_ve, v_den)], inc=s_ve)
            pe_b = mk(kb, "tensor",
                      lambda e: e.matmul(psB[:, :], ones_f[:, :], drow[0:1, 512:1024],
                                         start=True, stop=True),
                      waits=[(s_ve, max(v_rcp, last_psB_ve))], inc=s_pe)
            v_rc = mk(kb, "vector",
                      lambda e: e.tensor_copy(Rcp[:, :], psB[:, :]),
                      waits=[(s_pe, pe_b)], inc=s_ve)
            last_psB_ve = v_rc
            ob = qb
            v_oev = mk(kb, "vector",
                       lambda e, qt=qt, ob=ob: e.tensor_tensor(
                           ost[:, ob * 512:(ob + 1) * 512], psO[qt % 2][:, :],
                           Rcp[:, :], op=OP.mult),
                       waits=[(s_ost[ob], 16 * n_ost[ob])], inc=s_ve)
            last_O_ve[qt % 2] = v_oev
            n_ost[ob] += 1
            mk(kb, "gpsimd",
               lambda e, lh=lh, qt=qt, ob=ob: e.dma_start(
                   out=bass.AP(og_srcs[lh].tensor, qt * 512, [[T, 128], [1, 512]]),
                   in_=ost[:, ob * 512:(ob + 1) * 512]),
               waits=[(s_ve, v_oev)], inc=s_ost[ob], k=16)

    attn_done = [(s_ost[0], 16 * n_ost[0]), (s_ost[1], 16 * n_ost[1])]

    for nm in ("psB", "psDen1", "psDen0", "psO1", "psO0", "psS1", "psS0",
               "ost", "Rcp", "drow", "p_sb", "vgrp", "kgrp", "qtile", "s_b_tri_guard"):
        if nm in kb._guards:
            kb.free(nm)
    kb.free("tri_sb")

    s_cc = kb.sem("s_cc")
    s_og = kb.sem("s_og")
    s_tok = kb.sem("s_tok")
    s_sc = kb.sem("s_sc")
    s_wo = [kb.sem("s_wo0"), kb.sem("s_wo1")]
    s_yo = [kb.sem("s_yo0"), kb.sem("s_yo1")]
    s_ld = [kb.sem("s_ld0"), kb.sem("s_ld1")]
    s_st = [kb.sem("s_st0"), kb.sem("s_st1")]
    s_gw = [kb.sem("s_gw0"), kb.sem("s_gw1")]
    s_dw = [kb.sem("s_dw0"), kb.sem("s_dw1")]
    s_yd = [kb.sem("s_yd0"), kb.sem("s_yd1")]
    s_lnd = kb.sem("s_lnd")

    BAR2 = [(s_pe, s_pe.n), (s_ve, s_ve.n), (s_ac, s_ac.n)] + attn_done + \
           [(s_qt, 16 * n_qld),
            (s_kg[0], 16 * n_kg[0]), (s_kg[1], 16 * n_kg[1]),
            (s_vg[0], 16 * n_vg[0]), (s_vg[1], 16 * n_vg[1])]
    bar2_used = {e: False for e in ENGINES}
    def bar2w(eng, extra=()):
        w = list(extra)
        if not bar2_used[eng]:
            w = BAR2 + w
            bar2_used[eng] = True
        return w

    rows2 = kb.sb("rows2", [1, 512 * 4], F32)
    R2 = dict(m=rows2[0:1, 0:512], e=rows2[0:1, 512:1024], v=rows2[0:1, 512:1024],
              r=rows2[0:1, 1024:1536], mr=rows2[0:1, 1536:2048], t=rows2[0:1, 1024:1536])
    Rbc2 = kb.sb("Rbc2", [128, TPC], F32)
    MRbc2 = kb.sb("MRbc2", [128, TPC], F32)
    scl_sb = kb.sb("scl_sb", [128, 2 * 48 + 16], F32)
    ybuf = kb.sb("ybuf", [128, TPC], F32)
    ctmp = kb.sb("ctmp", [128, TPC], F32R)
    sqtmp = kb.sb("sqtmp", [128, TPC], F32R)
    tok_sb = kb.sb("tok_sb", [1, 2], mybir.dt.int32)
    ones_fr = kb.sb("ones_fr", [128, 1], F32R)
    runrows = kb.sb("runrows", [1, 4 * 512], F32)
    mk(kb, "vector", lambda e: e.tensor_copy(ones_fr[:, :], cst_sb[:, 3:4]), inc=s_ve)
    fuse = kb.sb("fuse", [128, 48 * TPC], BF16)

    greg = kb.es.enter_context(nc.gpsimd.register("tokreg"))
    mk(kb, "gpsimd", lambda e: e.dma_start(out=tok_sb[0:1, 0:1], in_=tokb[:, :]),
       waits=bar2w("gpsimd"), inc=s_tok, k=16)
    def _ldreg(e):
        e.wait_ge(s_tok.h, 16)
        e.reg_load(greg, tok_sb[0:1, 0:1])
    kb.ops["gpsimd"].append(_ldreg)
    mk(kb, "sync", lambda e: e.dma_start(out=scl_sb[:, 0:48], in_=gsT[:, :]),
       waits=bar2w("sync"), inc=s_sc, k=16)
    mk(kb, "sync", lambda e: e.dma_start(out=scl_sb[:, 48:96], in_=usT[:, :]), inc=s_sc, k=16)
    mk(kb, "sync", lambda e: e.dma_start(out=scl_sb[:, 96:112], in_=dsT[:, :]), inc=s_sc, k=16)
    for lh in range(HPC):
        mk(kb, "gpsimd",
           lambda e, lh=lh: e.collective_compute(
               "AllGather", OP.bypass,
               replica_groups=[[0, 1, 2, 3], [4, 5, 6, 7]],
               ins=[og_srcs[lh].opt()], outs=[og_dsts[lh].opt()]),
           waits=attn_done if lh == 0 else (), inc=s_cc)

    og_all = kb.sb("og_all", [128, ECH * TPC], BF16)
    wostg = kb.sb("wostg", [128, 2 * 2048], F32)
    wob = kb.sb("wob", [128, 2 * 2048], BF16)
    psSx = [kb.ps("psSx0", [1, 512]), kb.ps("psSx1", [1, 512])]
    psSxx = [kb.ps("psSxx0", [1, 512]), kb.ps("psSxx1", [1, 512])]
    psBB = kb.ps("psBB", [128, 512])
    psY = [kb.ps("psY0", [128, 512]), kb.ps("psY1", [128, 512])]

    def _og_load(e):
        e.wait_ge(s_cc.h, HPC)
        def dmas(base):
            for lh in range(HPC):
                for cp in range(4):
                    e.dma_start(
                        out=bass.AP(og_all[:, :].tensor,
                                    og_all[:, :].offset + (lh * 4 + cp) * TPC,
                                    [[ECH * TPC, 128], [1, TPC]]),
                        in_=bass.AP(og_dsts[lh].tensor, base + cp * 128 * T,
                                    [[T, 128], [1, TPC]]),
                    ).then_inc(s_og.h, 16)
        with e.If_eq(greg, 0):
            dmas(0)
        with e.Else():
            with e.If_eq(greg, TPC):
                dmas(TPC)
            with e.Else():
                with e.If_eq(greg, 2 * TPC):
                    dmas(2 * TPC)
                with e.Else():
                    dmas(3 * TPC)
    kb.ops["gpsimd"].append(_og_load)
    s_og.n += 16 * HPC * 4
    OGL = (s_og, 16 * HPC * 4)

    n_wo = [0, 0]; n_yo = [0, 0]; n_ld = [0, 0]; n_st = [0, 0]
    n_gw = [0, 0]; n_dw = [0, 0]; n_yd = [0, 0]
    last_wocast = [0, 0]
    last_psY_ac = [0, 0]
    last_ybuf_done = [0, 0]
    last_ctmp_pe = [0, 0]
    last_stats_ve = 0

    def emit_rows(sx, sxx, Rdst, MRdst, gate_pe, bb):
        vv = 0
        for th in range(2):
            mk(kb, "vector", lambda e, th=th: e.tensor_scalar_mul(
                R2["m"], runrows[0:1, th * 512:(th + 1) * 512], 1.0 / E), inc=s_ve)
            mk(kb, "vector", lambda e, th=th: e.tensor_scalar_mul(
                R2["e"], runrows[0:1, 1024 + th * 512:1024 + (th + 1) * 512], 1.0 / E), inc=s_ve)
            mk(kb, "vector", lambda e: e.tensor_tensor(R2["t"], R2["m"], R2["m"], op=OP.mult), inc=s_ve)
            mk(kb, "vector", lambda e: e.tensor_tensor(R2["v"], R2["e"], R2["t"], op=OP.subtract), inc=s_ve)
            v_var = mk(kb, "vector", lambda e: e.tensor_scalar_max(R2["v"], R2["v"], 0.0), inc=s_ve)
            a_s = mk(kb, "scalar", lambda e: e.activation(out=R2["v"], in_=R2["v"], func=AF.Sqrt,
                                                          bias=cst_sb[0:1, 0:1]),
                     waits=[(s_ve, v_var)], inc=s_ac)
            mk(kb, "vector", lambda e: e.reciprocal(R2["r"], R2["v"]), waits=[(s_ac, a_s)], inc=s_ve)
            mk(kb, "vector", lambda e: e.tensor_tensor(R2["mr"], R2["m"], R2["r"], op=OP.mult), inc=s_ve)
            for (key, dst) in (("r", Rdst), ("mr", MRdst)):
                pe_b2 = mk(kb, "tensor",
                           lambda e, key=key, bb=bb: e.matmul(bb[:, :], ones_f[:, :], R2[key],
                                                              start=True, stop=True),
                           waits=[(s_ve, s_ve.n)], inc=s_pe)
                vv = mk(kb, "vector",
                        lambda e, dst=dst, th=th, bb=bb: e.tensor_copy(dst[:, th * 512:(th + 1) * 512],
                                                                       bb[:, :]),
                        waits=[(s_pe, pe_b2)], inc=s_ve)
        return vv

    for gc in range(ECH):
        gl = gc % 2
        n_wo[gl] += 1
        mk(kb, "sync",
           lambda e, gc=gc, gl=gl: e.dma_start(
               out=bass.AP(wostg[:, :].tensor, wostg[:, :].offset + gl * 2048,
                           [[4096, 128], [128, 16], [1, 128]]),
               in_=bass.AP(woTp.tensor, gc * 128, [[2048, 128], [128 * 2048, 16], [1, 128]])),
           waits=[(s_ve, last_wocast[gl])], inc=s_wo[gl], k=16)
        v_woc = mk(kb, "vector",
                   lambda e, gl=gl: e.tensor_copy(wob[:, gl * 2048:(gl + 1) * 2048],
                                                  wostg[:, gl * 2048:(gl + 1) * 2048]),
                   waits=bar2w("vector", [(s_wo[gl], 16 * n_wo[gl]),
                                          (s_pe, max(last_ctmp_pe))]),
                   inc=s_ve)
        last_wocast[gl] = v_woc
        for th in range(2):
            for b in range(16):
                mk(kb, "tensor",
                   lambda e, gl=gl, b=b, th=th: e.matmul(
                       psY[th][:, :],
                       wob[:, gl * 2048 + b * 128:gl * 2048 + b * 128 + 128],
                       og_all[:, b * TPC + th * 512:b * TPC + th * 512 + 512],
                       start=(b == 0), stop=(b == 15)),
                   waits=(bar2w("tensor", [(s_ve, v_woc), OGL, (s_ac, last_psY_ac[th])])
                          if b == 0 else ()),
                   inc=(s_pe if b == 15 else None))
            pe_y = s_pe.n
            a_ev = mk(kb, "scalar",
                      lambda e, th=th, gl=gl: e.copy(
                          ybuf[:, th * 512:(th + 1) * 512],
                          psY[th][:, :]),
                      waits=bar2w("scalar", [(s_pe, pe_y), (s_yo[0], 16 * n_yo[0]),
                                             (s_yo[1], 16 * n_yo[1]),
                                             (s_ve, max(last_ybuf_done))]),
                      inc=s_ac)
            last_psY_ac[th] = a_ev
        a_ev2 = last_psY_ac[1]
        v_c = mk(kb, "vector",
                 lambda e, gl=gl: e.tensor_copy(ctmp[:, 0:TPC],
                                                ybuf[:, 0:TPC]),
                 waits=[(s_ac, a_ev2), (s_pe, max(last_ctmp_pe))], inc=s_ve)
        last_ybuf_done[gl] = v_c
        a_sq2 = mk(kb, "scalar",
                   lambda e, gl=gl: e.activation(out=sqtmp[:, 0:TPC],
                                                 in_=ybuf[:, 0:TPC],
                                                 func=AF.Square),
                   waits=[(s_ve, v_c)], inc=s_ac)
        last_sq_prev = a_sq2
        for th in range(2):
            mk(kb, "tensor",
               lambda e, th=th, ps=psSx[th]: e.matmul(
                   ps[:, :], ones_fr[:, :],
                   ctmp[:, th * 512:(th + 1) * 512],
                   start=True, stop=True),
               waits=[(s_ve, v_c), (s_ac, a_sq2), (s_ve, last_stats_ve)])
            mk(kb, "tensor",
               lambda e, th=th, ps=psSxx[th]: e.matmul(
                   ps[:, :], ones_fr[:, :],
                   sqtmp[:, th * 512:(th + 1) * 512],
                   start=True, stop=True),
               inc=(s_pe if th == 1 else None))
        pe_st = s_pe.n
        for th in range(2):
            op0 = "copy" if gc == 0 else "add"
            if gc == 0:
                mk(kb, "vector", lambda e, th=th, ps=psSx[th]: e.tensor_copy(
                    runrows[0:1, th * 512:(th + 1) * 512], ps[:, :]),
                   waits=[(s_pe, pe_st)] if th == 0 else (), inc=s_ve)
                mk(kb, "vector", lambda e, th=th, ps=psSxx[th]: e.tensor_copy(
                    runrows[0:1, 1024 + th * 512:1024 + (th + 1) * 512], ps[:, :]),
                   inc=s_ve)
            else:
                mk(kb, "vector", lambda e, th=th, ps=psSx[th]: e.tensor_tensor(
                    runrows[0:1, th * 512:(th + 1) * 512],
                    runrows[0:1, th * 512:(th + 1) * 512], ps[:, :], op=OP.add),
                   waits=[(s_pe, pe_st)] if th == 0 else (), inc=s_ve)
                mk(kb, "vector", lambda e, th=th, ps=psSxx[th]: e.tensor_tensor(
                    runrows[0:1, 1024 + th * 512:1024 + (th + 1) * 512],
                    runrows[0:1, 1024 + th * 512:1024 + (th + 1) * 512], ps[:, :], op=OP.add),
                   inc=s_ve)
        last_stats_ve = s_ve.n
        last_ctmp_pe[gl] = s_pe.n
        n_yo[gl] += 1
        mk(kb, "gpsimd",
           lambda e, gc=gc, gl=gl: e.dma_start(
               out=bass.AP(yo_dram.tensor, gc * 128 * TPC, [[TPC, 128], [1, TPC]]),
               in_=ybuf[:, 0:TPC]),
           waits=[(s_ac, a_ev2)], inc=s_yo[gl], k=16)
    pe_stats2 = s_pe.n
    last_stats_ve = emit_rows(psSx, psSxx, Rbc2, MRbc2, pe_stats2, psBB)
    if debug:
        mk(kb, "gpsimd", lambda e: e.dma_start(out=dbg["r2"][:, 0:TPC], in_=Rbc2[:, :]),
           waits=[(s_ve, last_stats_ve)], inc=s_fin, k=16)
        mk(kb, "gpsimd", lambda e: e.dma_start(out=dbg["r2"][:, TPC:2 * TPC], in_=MRbc2[:, :]),
           inc=s_fin, k=16)
        mk(kb, "gpsimd", lambda e: e.dma_start(out=dbg["rows"][0:1, 0:2048],
                                               in_=bass.AP(runrows[:, :].tensor, runrows[:, :].offset,
                                                           [[2048, 1], [1, 2048]])),
           inc=s_fin, k=16)

    kb.free("wob"); kb.free("wostg"); kb.free("og_all")
    h2_sb = kb.sb("h2_sb", [128, ECH * TPC], BF16)
    ych = kb.sb("ych", [128, 2 * TPC], F32)
    xch = kb.sb("xch", [128, 2 * TPC], F32)
    tt1 = kb.sb("tt1", [128, TPC], F32)
    tt2 = kb.sb("tt2", [128, TPC], F32)
    yo_done = [(s_yo[0], 16 * n_yo[0]), (s_yo[1], 16 * n_yo[1])]
    if debug:
        yo_done = yo_done + [(s_fin, s_fin.n)]
    last_sq_prev = 0
    last_tt_use = [0, 0]
    for gc in range(ECH):
        gl = gc % 2
        n_ld[gl] += 2
        mk(kb, "gpsimd",
           lambda e, gc=gc, gl=gl: e.dma_start(
               out=ych[:, gl * TPC:(gl + 1) * TPC],
               in_=bass.AP(yo_dram.tensor, gc * 128 * TPC, [[TPC, 128], [1, TPC]])),
           waits=(yo_done if gc < 2 else []) + [(s_ve, last_tt_use[gl])],
           inc=s_ld[gl], k=16)
        mk(kb, "gpsimd",
           lambda e, gc=gc, gl=gl: e.dma_start(
               out=xch[:, gl * TPC:(gl + 1) * TPC],
               in_=bass.AP(xownT.tensor, gc * 128 * TPC, [[TPC, 128], [1, TPC]])),
           inc=s_ld[gl], k=16)
        LD = (s_ld[gl], 16 * n_ld[gl])
        mk(kb, "vector",
           lambda e, gl=gl: e.tensor_tensor(tt1[:, :], ych[:, gl * TPC:(gl + 1) * TPC],
                                            Rbc2[:, :], op=OP.mult),
           waits=[LD, (s_ve, last_stats_ve)], inc=s_ve)
        mk(kb, "vector", lambda e: e.tensor_tensor(tt2[:, :], tt1[:, :], MRbc2[:, :],
                                                   op=OP.subtract), inc=s_ve)
        v_s = mk(kb, "vector",
                 lambda e, gl=gl: e.tensor_tensor(ybuf[:, 0:TPC], tt2[:, :],
                                                  xch[:, gl * TPC:(gl + 1) * TPC], op=OP.add),
                 waits=[(s_st[0], 16 * n_st[0]), (s_st[1], 16 * n_st[1]),
                        (s_ac, last_sq_prev)] +
                       (yo_done if gc < 2 else []), inc=s_ve)
        last_tt_use[gl] = v_s
        v_c = mk(kb, "vector",
                 lambda e, gl=gl: e.tensor_copy(ctmp[:, 0:TPC],
                                                ybuf[:, 0:TPC]),
                 waits=[(s_pe, max(last_ctmp_pe))], inc=s_ve)
        a_sq2 = mk(kb, "scalar",
                   lambda e, gl=gl: e.activation(out=sqtmp[:, 0:TPC],
                                                 in_=ybuf[:, 0:TPC],
                                                 func=AF.Square),
                   waits=[(s_ve, v_c)], inc=s_ac)
        last_sq_prev = a_sq2
        for th in range(2):
            mk(kb, "tensor",
               lambda e, th=th, ps=psSx[th]: e.matmul(
                   ps[:, :], ones_fr[:, :],
                   ctmp[:, th * 512:(th + 1) * 512],
                   start=True, stop=True),
               waits=[(s_ve, v_c), (s_ac, a_sq2), (s_ve, last_stats_ve)])
            mk(kb, "tensor",
               lambda e, th=th, ps=psSxx[th]: e.matmul(
                   ps[:, :], ones_fr[:, :],
                   sqtmp[:, th * 512:(th + 1) * 512],
                   start=True, stop=True),
               inc=(s_pe if th == 1 else None))
        pe_st = s_pe.n
        for th in range(2):
            op0 = "copy" if gc == 0 else "add"
            if gc == 0:
                mk(kb, "vector", lambda e, th=th, ps=psSx[th]: e.tensor_copy(
                    runrows[0:1, th * 512:(th + 1) * 512], ps[:, :]),
                   waits=[(s_pe, pe_st)] if th == 0 else (), inc=s_ve)
                mk(kb, "vector", lambda e, th=th, ps=psSxx[th]: e.tensor_copy(
                    runrows[0:1, 1024 + th * 512:1024 + (th + 1) * 512], ps[:, :]),
                   inc=s_ve)
            else:
                mk(kb, "vector", lambda e, th=th, ps=psSx[th]: e.tensor_tensor(
                    runrows[0:1, th * 512:(th + 1) * 512],
                    runrows[0:1, th * 512:(th + 1) * 512], ps[:, :], op=OP.add),
                   waits=[(s_pe, pe_st)] if th == 0 else (), inc=s_ve)
                mk(kb, "vector", lambda e, th=th, ps=psSxx[th]: e.tensor_tensor(
                    runrows[0:1, 1024 + th * 512:1024 + (th + 1) * 512],
                    runrows[0:1, 1024 + th * 512:1024 + (th + 1) * 512], ps[:, :], op=OP.add),
                   inc=s_ve)
        last_stats_ve = s_ve.n
        last_ctmp_pe[gl] = s_pe.n
        n_st[gl] += 1
        mk(kb, "gpsimd",
           lambda e, gc=gc, gl=gl: e.dma_start(
               out=bass.AP(st_dram.tensor, gc * 128 * TPC, [[TPC, 128], [1, TPC]]),
               in_=ybuf[:, 0:TPC]),
           waits=[(s_ve, v_s)], inc=s_st[gl], k=16)
    pe_stats3 = s_pe.n
    last_stats_ve = emit_rows(psSx, psSxx, Rbc2, MRbc2, pe_stats3, psBB)

    st_done = [(s_st[0], 16 * n_st[0]), (s_st[1], 16 * n_st[1])]
    for gc in range(ECH):
        gl = gc % 2
        n_ld[gl] += 1
        mk(kb, "gpsimd",
           lambda e, gc=gc, gl=gl: e.dma_start(
               out=ych[:, gl * TPC:(gl + 1) * TPC],
               in_=bass.AP(st_dram.tensor, gc * 128 * TPC, [[TPC, 128], [1, TPC]])),
           waits=(st_done if gc < 2 else []) + [(s_ve, last_tt_use[gl])],
           inc=s_ld[gl], k=16)
        LD = (s_ld[gl], 16 * n_ld[gl])
        mk(kb, "vector",
           lambda e, gl=gl: e.tensor_tensor(tt1[:, :], ych[:, gl * TPC:(gl + 1) * TPC],
                                            Rbc2[:, :], op=OP.mult),
           waits=[LD, (s_ve, last_stats_ve)], inc=s_ve)
        mk(kb, "vector",
           lambda e: e.tensor_tensor(tt2[:, :], tt1[:, :], MRbc2[:, :], op=OP.subtract),
           inc=s_ve)
        v_prev = s_ve.n
        a_h2 = mk(kb, "scalar",
                  lambda e, gc=gc: e.copy(h2_sb[:, gc * TPC:(gc + 1) * TPC], tt2[:, :]),
                  waits=[(s_ve, v_prev), (s_ac, 0)], inc=s_ac)
        v_h2 = mk(kb, "vector", lambda e, a_h2=a_h2: e.engine_nop(),
                  waits=[(s_ac, a_h2)], inc=s_ve)
        last_tt_use[gl] = v_h2
        if debug and gc == 0:
            mk(kb, "gpsimd", lambda e: e.dma_start(out=dbg["c3"][:, 0:TPC], in_=tt1[:, :]),
               waits=[(s_ve, v_h2)], inc=s_fin, k=16)
            mk(kb, "gpsimd", lambda e: e.dma_start(out=dbg["c3"][:, TPC:2 * TPC], in_=tt2[:, :]),
               inc=s_fin, k=16)
            mk(kb, "gpsimd", lambda e, gl=gl: e.dma_start(out=dbg["c3"][:, 2 * TPC:3 * TPC],
                                                          in_=ych[:, gl * TPC:(gl + 1) * TPC]),
               inc=s_fin, k=16)
            gc0_dump = [(s_fin, s_fin.n)]
        if debug and gc == 1:
            mk(kb, "vector", lambda e: e.engine_nop(), waits=gc0_dump, inc=s_ve)
    v_h2_all = s_ve.n

    kb.free("tt2"); kb.free("tt1"); kb.free("xch"); kb.free("ych")
    for nm in ("psY1", "psY0", "psBB", "psSxx1", "psSxx0", "psSx1", "psSx0"):
        kb.free(nm)
    psGU = [[kb.ps("psGU%d%d" % (i, j), [128, 512]) for j in range(4)] for i in range(2)]
    gstg = kb.sb("gstg", [128, 2 * 2 * 2048], F32)
    g8 = kb.sb("g8", [128, 2 * 2 * 2048], F8)
    gact = kb.sb("gact", [128, TPC], BF16)

    last_gcast = [0, 0]
    last_g8_pe = [0, 0]
    last_fuse_ve = [0, 0]
    last_gact_ve = 0
    n_gwl = [0, 0]
    for ic in range(48):
        gl = ic % 2
        n_gwl[gl] += 2
        mk(kb, "sync",
           lambda e, ic=ic, gl=gl: e.dma_start(
               out=bass.AP(gstg[:, :].tensor, gstg[:, :].offset + gl * 4096,
                           [[8192, 128], [128, 16], [1, 128]]),
               in_=bass.AP(gw.tensor, ic * 128, [[II, 128], [128 * II, 16], [1, 128]])),
           waits=[(s_ve, max(last_gcast[gl], v_h2_all if ic < 2 else 0))],
           inc=s_gw[gl], k=16)
        mk(kb, "sync",
           lambda e, ic=ic, gl=gl: e.dma_start(
               out=bass.AP(gstg[:, :].tensor, gstg[:, :].offset + gl * 4096 + 2048,
                           [[8192, 128], [128, 16], [1, 128]]),
               in_=bass.AP(uw.tensor, ic * 128, [[II, 128], [128 * II, 16], [1, 128]])),
           inc=s_gw[gl], k=16)
        GLD = (s_gw[gl], 16 * n_gwl[gl])
        v_f8 = mk(kb, "vector",
                  lambda e, gl=gl: e.tensor_copy(g8[:, gl * 4096:(gl + 1) * 4096],
                                                 gstg[:, gl * 4096:(gl + 1) * 4096]),
                  waits=[GLD, (s_pe, last_g8_pe[gl])], inc=s_ve)
        last_gcast[gl] = v_f8
        for th in range(2):
            for eci in range(ECH):
                mk(kb, "tensor",
                   lambda e, gl=gl, th=th, eci=eci: e.matmul(
                       psGU[gl][th][:, :],
                       g8[:, gl * 4096 + eci * 128:gl * 4096 + eci * 128 + 128],
                       h2_sb[:, eci * TPC + th * 512:eci * TPC + (th + 1) * 512],
                       start=(eci == 0), stop=(eci == ECH - 1)),
                   waits=([(s_ve, v_f8), (s_ve, last_fuse_ve[gl])]
                          if (th == 0 and eci == 0) else ()))
            for eci in range(ECH):
                mk(kb, "tensor",
                   lambda e, gl=gl, th=th, eci=eci: e.matmul(
                       psGU[gl][2 + th][:, :],
                       g8[:, gl * 4096 + 2048 + eci * 128:gl * 4096 + 2048 + eci * 128 + 128],
                       h2_sb[:, eci * TPC + th * 512:eci * TPC + (th + 1) * 512],
                       start=(eci == 0), stop=(eci == ECH - 1)),
                   inc=(s_pe if eci == ECH - 1 else None))
            pe_gu = s_pe.n
            if th == 1:
                last_g8_pe[gl] = pe_gu
            a_si = mk(kb, "scalar",
                      lambda e, gl=gl, th=th, ic=ic: e.activation(
                          out=gact[:, th * 512:(th + 1) * 512],
                          in_=psGU[gl][th][:, :], func=AF.Silu,
                          scale=scl_sb[:, ic:ic + 1]),
                      waits=[(s_pe, pe_gu), (s_ve, last_gact_ve)] +
                            ([(s_sc, 48)] if ic < 2 else []), inc=s_ac)
            mk(kb, "vector",
               lambda e, gl=gl, th=th: e.tensor_tensor(
                   ctmp[:, th * 512:(th + 1) * 512],
                   psGU[gl][2 + th][:, :],
                   gact[:, th * 512:(th + 1) * 512], op=OP.mult),
               waits=[(s_ac, a_si)], inc=s_ve)
            mk(kb, "vector",
               lambda e, th=th, ic=ic: e.tensor_scalar_mul(
                   sqtmp[:, th * 512:(th + 1) * 512],
                   ctmp[:, th * 512:(th + 1) * 512],
                   scl_sb[:, 48 + ic:48 + ic + 1]),
               inc=s_ve)
            v_fu = mk(kb, "vector",
                      lambda e, th=th, ic=ic: e.tensor_copy(
                          fuse[:, ic * TPC + th * 512:ic * TPC + (th + 1) * 512],
                          sqtmp[:, th * 512:(th + 1) * 512]),
                      inc=s_ve)
            last_gact_ve = v_fu
            if debug and ic == 0 and th == 0:
                mk(kb, "vector", lambda e: e.tensor_copy(ybuf[:, 0:512], psGU[0][2][:, :]),
                   waits=[(s_ve, v_fu)], inc=s_ve)
                mk(kb, "vector", lambda e: e.tensor_copy(ybuf[:, 512:1024], gact[:, 0:512]), inc=s_ve)
                v_c4 = s_ve.n
                mk(kb, "gpsimd", lambda e: e.dma_start(out=dbg["c4"][:, 0:1024], in_=ybuf[:, 0:TPC]),
                   waits=[(s_ve, v_c4)], inc=s_fin, k=16)
                mk(kb, "gpsimd", lambda e: e.dma_start(out=dbg["c4"][:, 1024:1536],
                                                       in_=ctmp[:, 0:512]), inc=s_fin, k=16)
                mk(kb, "gpsimd", lambda e: e.dma_start(out=dbg["c4"][:, 1536:2048],
                                                       in_=sqtmp[:, 0:512]), inc=s_fin, k=16)
                mk(kb, "gpsimd", lambda e: e.dma_start(out=dbg["g8"][:, :], in_=g8[:, 0:4096]),
                   inc=s_fin, k=16)
                mk(kb, "gpsimd", lambda e: e.dma_start(out=dbg["h2b"][:, :], in_=h2_sb[:, 0:2048]),
                   inc=s_fin, k=16)
                ic0_gate = [(s_fin, s_fin.n)]
            if debug and ic == 0 and th == 1:
                mk(kb, "vector", lambda e: e.engine_nop(), waits=ic0_gate, inc=s_ve)
        last_fuse_ve[gl] = s_ve.n
    fuse_done_ve = s_ve.n

    if debug:
        mk(kb, "gpsimd", lambda e: e.dma_start(out=dbg["fuse"][:, :], in_=fuse[:, :]),
           waits=[(s_ve, fuse_done_ve)], inc=s_fin, k=16)
        mk(kb, "gpsimd", lambda e: e.dma_start(out=dbg["h2"][:, :], in_=h2_sb[:, :]),
           waits=[(s_ve, v_h2_all)], inc=s_fin, k=16)

    kb.free("gact"); kb.free("g8"); kb.free("gstg")
    for i in range(1, -1, -1):
        for j in range(3, -1, -1):
            kb.free("psGU%d%d" % (i, j))
    kb.free("h2_sb")
    psY2 = [kb.ps("psY2_0", [128, 512]), kb.ps("psY2_1", [128, 512])]
    psSx_b = [kb.ps("psSx0b", [1, 512]), kb.ps("psSx1b", [1, 512])]
    psSxx_b = [kb.ps("psSxx0b", [1, 512]), kb.ps("psSxx1b", [1, 512])]
    psBB_b = kb.ps("psBBb", [128, 512])
    dstg = kb.sb("dstg", [128, 2 * II], F32)
    d8 = kb.sb("d8", [128, 2 * II], F8)

    dbg_dump_gate = [(s_fin, s_fin.n)] if debug else []
    last_dcast = [0, 0]
    last_d8_pe = [0, 0]
    last_psY2_ac = [0, 0]
    last_ctmp_pe = [0, 0]
    last_ybuf_done = [0, 0]
    n_dwl = [0, 0]
    for gc in range(ECH):
        gl = gc % 2
        n_dwl[gl] += 1
        mk(kb, "sync",
           lambda e, gc=gc, gl=gl: e.dma_start(
               out=bass.AP(dstg[:, :].tensor, dstg[:, :].offset + gl * II,
                           [[2 * II, 128], [128, 48], [1, 128]]),
               in_=bass.AP(dw.tensor, gc * 128, [[E, 128], [128 * E, 48], [1, 128]])),
           waits=[(s_ve, max(last_dcast[gl], fuse_done_ve if gc < 2 else 0))] +
                 (dbg_dump_gate if gc < 2 else []),
           inc=s_dw[gl], k=16)
        DLD = (s_dw[gl], 16 * n_dwl[gl])
        v_f8 = mk(kb, "vector",
                  lambda e, gl=gl: e.tensor_copy(d8[:, gl * II:(gl + 1) * II],
                                                 dstg[:, gl * II:(gl + 1) * II]),
                  waits=[DLD, (s_pe, last_d8_pe[gl])], inc=s_ve)
        last_dcast[gl] = v_f8
        for th in range(2):
            for icc in range(48):
                mk(kb, "tensor",
                   lambda e, gl=gl, th=th, icc=icc: e.matmul(
                       psY2[th][:, :],
                       d8[:, gl * II + icc * 128:gl * II + icc * 128 + 128],
                       fuse[:, icc * TPC + th * 512:icc * TPC + (th + 1) * 512],
                       start=(icc == 0), stop=(icc == 47)),
                   waits=([(s_ve, v_f8), (s_ac, last_psY2_ac[th]), (s_ve, fuse_done_ve)]
                          if icc == 0 else ()),
                   inc=(s_pe if icc == 47 else None))
            pe_y2 = s_pe.n
            if th == 1:
                last_d8_pe[gl] = pe_y2
            a_ev = mk(kb, "scalar",
                      lambda e, th=th, gl=gl, gc=gc: e.activation(
                          out=ybuf[:, th * 512:(th + 1) * 512],
                          in_=psY2[th][:, :], func=AF.Copy,
                          scale=scl_sb[:, 96 + gc:96 + gc + 1]),
                      waits=[(s_pe, pe_y2), (s_yd[0], 16 * n_yd[0]),
                             (s_yd[1], 16 * n_yd[1]),
                             (s_ve, max(last_ybuf_done))],
                      inc=s_ac)
            last_psY2_ac[th] = a_ev
        a_ev2 = last_psY2_ac[1]
        v_c = mk(kb, "vector",
                 lambda e, gl=gl: e.tensor_copy(ctmp[:, 0:TPC],
                                                ybuf[:, 0:TPC]),
                 waits=[(s_ac, a_ev2), (s_pe, max(last_ctmp_pe))], inc=s_ve)
        last_ybuf_done[gl] = v_c
        a_sq2 = mk(kb, "scalar",
                   lambda e, gl=gl: e.activation(out=sqtmp[:, 0:TPC],
                                                 in_=ybuf[:, 0:TPC],
                                                 func=AF.Square),
                   waits=[(s_ve, v_c)], inc=s_ac)
        for th in range(2):
            mk(kb, "tensor",
               lambda e, th=th, ps=psSx_b[th]: e.matmul(
                   ps[:, :], ones_fr[:, :],
                   ctmp[:, th * 512:(th + 1) * 512],
                   start=True, stop=True),
               waits=[(s_ve, v_c), (s_ac, a_sq2)])
            mk(kb, "tensor",
               lambda e, th=th, ps=psSxx_b[th]: e.matmul(
                   ps[:, :], ones_fr[:, :],
                   sqtmp[:, th * 512:(th + 1) * 512],
                   start=True, stop=True),
               inc=(s_pe if th == 1 else None))
        pe_st = s_pe.n
        for th in range(2):
            op0 = "copy" if gc == 0 else "add"
            if gc == 0:
                mk(kb, "vector", lambda e, th=th, ps=psSx_b[th]: e.tensor_copy(
                    runrows[0:1, th * 512:(th + 1) * 512], ps[:, :]),
                   waits=[(s_pe, pe_st)] if th == 0 else (), inc=s_ve)
                mk(kb, "vector", lambda e, th=th, ps=psSxx_b[th]: e.tensor_copy(
                    runrows[0:1, 1024 + th * 512:1024 + (th + 1) * 512], ps[:, :]),
                   inc=s_ve)
            else:
                mk(kb, "vector", lambda e, th=th, ps=psSx_b[th]: e.tensor_tensor(
                    runrows[0:1, th * 512:(th + 1) * 512],
                    runrows[0:1, th * 512:(th + 1) * 512], ps[:, :], op=OP.add),
                   waits=[(s_pe, pe_st)] if th == 0 else (), inc=s_ve)
                mk(kb, "vector", lambda e, th=th, ps=psSxx_b[th]: e.tensor_tensor(
                    runrows[0:1, 1024 + th * 512:1024 + (th + 1) * 512],
                    runrows[0:1, 1024 + th * 512:1024 + (th + 1) * 512], ps[:, :], op=OP.add),
                   inc=s_ve)
        last_stats_ve = s_ve.n
        last_ctmp_pe[gl] = s_pe.n
        n_yd[gl] += 1
        mk(kb, "gpsimd",
           lambda e, gc=gc, gl=gl: e.dma_start(
               out=bass.AP(y_dram.tensor, gc * 128 * TPC, [[TPC, 128], [1, TPC]]),
               in_=ybuf[:, 0:TPC]),
           waits=[(s_ac, a_ev2)], inc=s_yd[gl], k=16)
    pe_stats4 = s_pe.n
    last_stats_ve = emit_rows(psSx_b, psSxx_b, Rbc2, MRbc2, pe_stats4, psBB_b)

    kb.free("d8"); kb.free("dstg")
    ych_f = kb.sb("ych2", [128, 2 * TPC], F32)
    xch_f = kb.sb("xch2", [128, 2 * TPC], F32)
    tt1_f = kb.sb("tt1b", [128, TPC], F32)
    tt2_f = kb.sb("tt2b", [128, TPC], F32)
    y_done = [(s_yd[0], 16 * n_yd[0]), (s_yd[1], 16 * n_yd[1])]
    last_tt_use = [0, 0]
    last_out_st = [0, 0]
    for gc in range(ECH):
        gl = gc % 2
        n_ld[gl] += 2
        mk(kb, "gpsimd",
           lambda e, gc=gc, gl=gl: e.dma_start(
               out=ych_f[:, gl * TPC:(gl + 1) * TPC],
               in_=bass.AP(y_dram.tensor, gc * 128 * TPC, [[TPC, 128], [1, TPC]])),
           waits=(y_done if gc < 2 else []) + [(s_ve, last_tt_use[gl])],
           inc=s_ld[gl], k=16)
        mk(kb, "gpsimd",
           lambda e, gc=gc, gl=gl: e.dma_start(
               out=xch_f[:, gl * TPC:(gl + 1) * TPC],
               in_=bass.AP(st_dram.tensor, gc * 128 * TPC, [[TPC, 128], [1, TPC]])),
           inc=s_ld[gl], k=16)
        LD = (s_ld[gl], 16 * n_ld[gl])
        mk(kb, "vector",
           lambda e, gl=gl: e.tensor_tensor(tt1_f[:, :], ych_f[:, gl * TPC:(gl + 1) * TPC],
                                            Rbc2[:, :], op=OP.mult),
           waits=[LD, (s_ve, last_stats_ve)], inc=s_ve)
        mk(kb, "vector", lambda e: e.tensor_tensor(tt2_f[:, :], tt1_f[:, :], MRbc2[:, :],
                                                   op=OP.subtract), inc=s_ve)
        v_o = mk(kb, "vector",
                 lambda e, gl=gl: e.tensor_tensor(ybuf[:, 0:TPC], tt2_f[:, :],
                                                  xch_f[:, gl * TPC:(gl + 1) * TPC], op=OP.add),
                 waits=[(s_fin, max(last_out_st)), (s_yd[0], 16 * n_yd[0]),
                        (s_yd[1], 16 * n_yd[1])], inc=s_ve)
        last_tt_use[gl] = v_o
        mk(kb, "gpsimd",
           lambda e, gc=gc, gl=gl: e.dma_start(
               out=bass.AP(outT.tensor, gc * 128 * TPC, [[TPC, 128], [1, TPC]]),
               in_=ybuf[:, 0:TPC]),
           waits=[(s_ve, v_o)], inc=s_fin, k=16)
        last_out_st[gl] = s_fin.n

    if debug:
        mk(kb, "gpsimd", lambda e: e.dma_start(out=dbg["qT"][:, :], in_=q_dram[:, :]),
           waits=[(s_ve, s_ve.n)] + all_dma_done, inc=s_fin, k=16)
        mk(kb, "gpsimd", lambda e: e.dma_start(out=dbg["kT"][:, :], in_=k_dram[:, :]), inc=s_fin, k=16)
        mk(kb, "gpsimd", lambda e: e.dma_start(out=dbg["bnd"][:, :], in_=bnd_sb[:, :]),
           waits=[(s_ve, v_bnd)], inc=s_fin, k=16)
        mk(kb, "gpsimd", lambda e: e.dma_start(out=dbg["v"][:, :], in_=v_dram[:, :]), inc=s_fin, k=16)
        for i4 in range(HPC):
            mk(kb, "gpsimd", lambda e, i4=i4: e.dma_start(
                out=dbg["oT"][i4 * 128:(i4 + 1) * 128, :], in_=og_srcs[i4][:, :]),
               inc=s_fin, k=16)
        mk(kb, "gpsimd", lambda e: e.dma_start(out=dbg["stream"][:, :], in_=st_dram[:, :]),
           inc=s_fin, k=16)
        mk(kb, "gpsimd", lambda e: e.dma_start(out=dbg["yo"][:, :], in_=yo_dram[:, :]),
           inc=s_fin, k=16)
        mk(kb, "gpsimd", lambda e: e.dma_start(out=dbg["ogd"][:, :], in_=og_dsts[0][:, :]),
           inc=s_fin, k=16)
        mk(kb, "gpsimd", lambda e: e.dma_start(out=dbg["y"][:, :], in_=y_dram[:, :]),
           inc=s_fin, k=16)

    def fin(engine, n=s_fin.n):
        engine.wait_ge(s_fin.h, n)
    kb.ops["gpsimd"].append(fin)

    kb.emit_all()
    return nc


def kernel(**inputs):
    maps = prep_in_maps(inputs)
    nc = build_nc(debug=False)
    res = run_bass_kernel_spmd(nc, maps, core_ids=list(range(8)))
    out = np.zeros((B_, T, E), np.float32)
    for c in range(8):
        b, g = divmod(c, 4)
        out[b, g * TPC:(g + 1) * TPC, :] = np.asarray(res.results[c]["outT"], np.float32).T
    return out
